# version 10
# baseline (speedup 1.0000x reference)
import numpy as np
import ml_dtypes
from contextlib import ExitStack

import concourse.bass as bass
import concourse.mybir as mybir
from concourse.bass_utils import run_bass_kernel_spmd

F32 = mybir.dt.float32
BF16 = mybir.dt.bfloat16
AF = mybir.ActivationFunctionType
ALU = mybir.AluOpType
AX = mybir.AxisListType

NCORES = 8
D = 2048
SEQ = 8192
TPC = 1024
NQB = 8
NH0 = 16
DH = 128
EPS = 1e-5
SCALE0 = float(1.0 / np.sqrt(np.sqrt(DH)))
ENGS = ("sync", "gpsimd", "tensor", "vector", "scalar")
DEBUG_STAGE = 99
DEBUG_TILES = 2


class SemT:
    __slots__ = ("h", "n", "name")

    def __init__(self, h, name):
        self.h = h
        self.n = 0
        self.name = name


class Ctx:
    def __init__(self, nc, stack):
        self.nc = nc
        self.stack = stack
        self.ops = {k: [] for k in ENGS}
        self.sems = {}
        self.ps = []
        self.sb = {}

    def sem(self, name):
        s = self.sems.get(name)
        if s is None:
            h = self.stack.enter_context(self.nc.semaphore(name))
            s = self.sems[name] = SemT(h, name)
        return s

    def emit(self, eng, f):
        self.ops[eng].append(f)

    def wait(self, eng, sem, val=None):
        v = sem.n if val is None else val
        if v > 0:
            self.emit(eng, lambda e, _s=sem, _v=v: e.wait_ge(_s.h, _v))

    def op(self, eng, f, sem=None, k=1, after=()):
        for a in after:
            self.wait(eng, a)
        if sem is not None:
            sem.n += k
            self.emit(eng, lambda e, _f=f, _s=sem, _k=k: _f(e).then_inc(_s.h, _k))
        else:
            self.emit(eng, f)

    def sig(self, eng, sem, k=1):
        sem.n += k
        self.emit(eng, lambda e, _s=sem, _k=k: e.nop().then_inc(_s.h, _k))

    def dma(self, eng, out, in_, sem, **kw):
        self.op(eng, lambda e, _o=out, _i=in_, _kw=kw: e.dma_start(out=_o, in_=_i, **_kw),
                sem, 16)

    def finalize(self):
        ops = self.ops
        with self.nc.Block() as block:
            @block.sync
            def _(e):
                for f in ops["sync"]:
                    f(e)

            @block.gpsimd
            def _(e):
                for f in ops["gpsimd"]:
                    f(e)

            @block.tensor
            def _(e):
                for f in ops["tensor"]:
                    f(e)

            @block.vector
            def _(e):
                for f in ops["vector"]:
                    f(e)

            @block.scalar
            def _(e):
                for f in ops["scalar"]:
                    f(e)


class PanelStream:

    def __init__(self, C, name, dma_eng="gpsimd", cons_eng="tensor", nbuf=2):
        self.C = C
        self.nbuf = nbuf
        self.ld = [C.sem(f"{name}_l{i}") for i in range(nbuf)]
        self.fr = C.sem(f"{name}_f")
        self.k = 0
        self.uk = 0
        self.pending = []
        self.dma_eng = dma_eng
        self.cons_eng = cons_eng

    def load(self, dmas):
        C = self.C
        k = self.k
        slot = k % self.nbuf
        if k >= self.nbuf:
            C.wait(self.dma_eng, self.fr, k - self.nbuf + 1)
        s = self.ld[slot]
        for d in dmas:
            dst, src = d[0], d[1]
            kw = d[2] if len(d) > 2 else {}
            C.dma(self.dma_eng, dst, src, s, **kw)
        self.pending.append((slot, s.n))
        self.k += 1

    def use(self, eng=None):
        slot, v = self.pending[0]
        self.C.wait(eng or self.cons_eng, self.ld[slot], v)

    def done(self, eng=None):
        self.C.sig(eng or self.cons_eng, self.fr, 1)
        self.uk += 1
        self.pending.pop(0)

    def done_attached(self):
        self.uk += 1
        self.pending.pop(0)


def alloc_common(C):
    nc = C.nc
    sb = C.sb
    sb["xr"] = nc.alloc_sbuf_tensor("xr", [128, NQB * D], F32)
    sb["xnf"] = nc.alloc_sbuf_tensor("xnf", [128, 16 * 1024], BF16)
    sb["abf"] = nc.alloc_sbuf_tensor("abf", [128, 37376], BF16)
    sb["bfs"] = nc.alloc_sbuf_tensor("bfs", [16, 2048], BF16)
    sb["bcol"] = nc.alloc_sbuf_tensor("bcol", [128, 64], BF16)
    sb["af32"] = nc.alloc_sbuf_tensor("af32", [128, 5120], F32)
    sb["mtri"] = nc.alloc_sbuf_tensor("mtri", [128, 8 * 128], BF16)
    sb["ident"] = nc.alloc_sbuf_tensor("ident", [128, 128], BF16)
    sb["onesc"] = nc.alloc_sbuf_tensor("onesc", [128, 1], BF16)
    sb["onesr"] = nc.alloc_sbuf_tensor("onesr", [1, 128], BF16)
    sb["pbias"] = nc.alloc_sbuf_tensor("pbias", [128, 96], F32)
    sb["stats"] = nc.alloc_sbuf_tensor("stats", [128, 64], F32)
    sb["h128"] = nc.alloc_sbuf_tensor("h128", [128, 128], BF16)
    sb["rope"] = nc.alloc_sbuf_tensor("rope", [128, 2048], BF16)
    C.pr = []
    for i in range(4):
        C.pr.append(nc.alloc_psum_tensor(f"pr{i}", [128, 1024], F32))
    for i in range(8):
        C.ps.append(PsView(C.pr[i // 2], (i % 2) * 512))


class PsView:
    def __init__(self, t, off):
        self.t = t
        self.off = off

    def ap(self):
        return self.t.ap()[:, self.off:self.off + 512]

    def bitcast(self, dt):
        return self.t.ap().bitcast(dt)[:, 2 * self.off: 2 * self.off + 1024]


def af32v(C, off, width):
    return C.sb["af32"][:, off:off + width]


def emit_ln(C, src_ap, sv, ss, out_ap=None, res_add=None, scratch_off=1024,
            presq=None, presum=None):
    st = C.sb["stats"]
    sq = af32v(C, scratch_off, D)
    ssq = st[:, 0:1]
    sx = st[:, 1:2]
    m = st[:, 2:3]
    var = st[:, 3:4]
    r = st[:, 4:5]
    b = st[:, 5:6]
    if presq is None:
        C.op("scalar", lambda e: e.activation(sq, src_ap, AF.Square, accum_out=ssq), ss,
             after=(ss,))
        C.op("vector", lambda e: e.reduce_sum(sx, src_ap, AX.X), sv, after=(sv,))
        C.wait("vector", ss)
    else:
        ssq, sx = presq, presum
    C.op("vector", lambda e: e.tensor_scalar_mul(m, sx, 1.0 / D), sv, after=(sv,))
    C.op("vector", lambda e: e.tensor_scalar_mul(var, ssq, 1.0 / D), sv, after=(sv,))
    C.op("vector", lambda e: e.tensor_tensor(b, m, m, ALU.mult), sv, after=(sv,))
    C.op("vector", lambda e: e.tensor_sub(var, var, b), sv, after=(sv,))
    C.op("vector", lambda e: e.tensor_scalar_add(var, var, EPS), sv, after=(sv,))
    C.wait("scalar", sv)
    C.op("scalar", lambda e: e.sqrt(var, var), ss)
    C.wait("vector", ss)
    C.op("vector", lambda e: e.reciprocal(r, var), sv, after=(sv,))
    C.op("vector", lambda e: e.tensor_tensor(b, m, r, ALU.mult), sv, after=(sv,))
    C.op("vector", lambda e: e.tensor_scalar_mul(b, b, -1.0), sv, after=(sv,))
    C.wait("scalar", sv)
    if res_add is None:
        C.op("scalar", lambda e: e.activation(out_ap, src_ap, AF.Identity,
                                              bias=b, scale=r), ss)
    else:
        C.op("scalar", lambda e: e.activation(sq, src_ap, AF.Identity,
                                              bias=b, scale=r), ss)
        C.wait("vector", ss)
        C.op("vector", lambda e: e.tensor_add(res_add, res_add, sq), sv, after=(sv,))


def emit_ln_pipe(C, items, sv, ss, bufs, stc0=16, sqdump=None, tr=None):
    st = C.sb["stats"]
    n = len(items)
    if sqdump is None:
        sqdump = bufs[0]
    stats = []
    for i, (src_ap, _) in enumerate(items):
        ssq = st[:, stc0 + 6 * i: stc0 + 6 * i + 1]
        sx = st[:, stc0 + 6 * i + 1: stc0 + 6 * i + 2]
        C.op("scalar", lambda e, _s=sqdump, _x=src_ap, _p=ssq:
             e.activation(_s, _x, AF.Square, accum_out=_p), ss, after=(ss,))
        C.op("vector", lambda e, _o=sx, _x=src_ap: e.reduce_sum(_o, _x, AX.X),
             sv, after=(sv,))
        stats.append((ssq, sx, ss.n, sv.n))
    buf_free = {}
    for i, (src_ap, dst) in enumerate(items):
        ssq, sx, ssm, svm = stats[i]
        cb = stc0 + 6 * i
        m = st[:, cb + 2: cb + 3]
        var = st[:, cb + 3: cb + 4]
        r = st[:, cb + 4: cb + 5]
        b = st[:, cb + 5: cb + 6]
        C.wait("vector", ss, ssm)
        C.op("vector", lambda e, _o=m, _x=sx: e.tensor_scalar_mul(_o, _x, 1.0 / D),
             sv, after=(sv,))
        C.op("vector", lambda e, _o=var, _x=ssq: e.tensor_scalar_mul(_o, _x, 1.0 / D),
             sv, after=(sv,))
        C.op("vector", lambda e, _o=b, _m=m: e.tensor_tensor(_o, _m, _m, ALU.mult),
             sv, after=(sv,))
        C.op("vector", lambda e, _o=var, _b=b: e.tensor_sub(_o, _o, _b),
             sv, after=(sv,))
        C.op("vector", lambda e, _o=var: e.tensor_scalar_add(_o, _o, EPS),
             sv, after=(sv,))
        C.wait("scalar", sv)
        C.op("scalar", lambda e, _v=var: e.sqrt(_v, _v), ss)
        C.wait("vector", ss)
        C.op("vector", lambda e, _o=r, _v=var: e.reciprocal(_o, _v),
             sv, after=(sv,))
        C.op("vector", lambda e, _o=b, _m=m, _r=r: e.tensor_tensor(_o, _m, _r, ALU.mult),
             sv, after=(sv,))
        C.op("vector", lambda e, _o=b: e.tensor_scalar_mul(_o, _o, -1.0),
             sv, after=(sv,))
        buf = bufs[i % len(bufs)]
        if i % len(bufs) in buf_free:
            _sm, _vv = buf_free[i % len(bufs)]
            C.wait("scalar", _sm, _vv)
        C.wait("scalar", sv)
        C.op("scalar", lambda e, _o=buf, _x=src_ap, _b=b, _r=r:
             e.activation(_o, _x, AF.Identity, bias=_b, scale=_r), ss)
        if tr is None:
            C.wait("vector", ss)
            C.op("vector", lambda e, _o=dst, _s=buf: e.tensor_add(_o, _o, _s),
                 sv, after=(sv,))
            buf_free[i % len(bufs)] = (sv, sv.n)
        else:
            buf_free[i % len(bufs)] = tr(i, buf)


def emit_ln1_fm(C, qbs, tag, sv, ss, stt):
    sb = C.sb
    xr, xnf, abf = sb["xr"], sb["xnf"], sb["abf"]
    XNTM = 32768
    tokw = 128 * len(qbs)
    evacs = []

    def tr(qi, buf):
        C.wait("tensor", ss)
        for ch in range(16):
            t = qi * 16 + ch
            bank = C.ps[6 + (t % 2)]
            pv = bank.bitcast(BF16)[:, 0:128]
            if len(evacs) >= 2:
                C.wait("tensor", sv, evacs[-2])
            C.op("tensor", lambda e, _p=pv, _i=buf[:, ch * 128:(ch + 1) * 128]:
                 e.transpose(_p, _i, sb["ident"][:, :]), stt)
            C.wait("vector", stt)
            dst = xnf[:, ch * tokw + qi * 128: ch * tokw + qi * 128 + 128]
            C.op("vector", lambda e, _o=dst, _p=pv: e.tensor_copy(_o, _p), sv)
            evacs.append(C.sems[sv.name].n)
        return (stt, stt.n)

    items = [(xr[:, qb * D:(qb + 1) * D], None) for qb in qbs]
    emit_ln_pipe(C, items, sv, ss,
                 bufs=[abf[:, XNTM:XNTM + D], abf[:, XNTM + D:XNTM + 2 * D]],
                 sqdump=af32v(C, 1024, D), tr=tr)


def emit_ffn(C, tag, gu_src, down_src, fwht, cast_gate=None):
    sb = C.sb
    xr, xnf, abf = sb["xr"], sb["xnf"], sb["abf"]
    NHC = 64
    HID_LO = 0
    WPA = 24576
    WPB = 28672
    SP0 = 34816
    sv = C.sem(tag + "v")
    ss = C.sem(tag + "s")
    stt = C.sem(tag + "t")

    def hidc(c):
        if c < 16:
            return xnf[:, 8192 + c * 512: 8192 + (c + 1) * 512]
        return abf[:, HID_LO + (c - 16) * 512: HID_LO + (c - 15) * 512]

    spare = [abf[:, SP0:SP0 + 512], abf[:, SP0 + 512:SP0 + 1024]]

    def guslot(slot):
        return abf[:, WPA + slot * 4096: WPA + (slot + 1) * 4096]

    ps_up = PanelStream(C, f"{tag}up")
    ps_dn = PanelStream(C, f"{tag}dn", nbuf=8)

    def dsl(c):
        off = WPA + (c % 8) * 1024
        return abf[:, off:off + 1024]

    for tile in range(DEBUG_TILES):
        qbs = [tile * 4 + i for i in range(4)]
        emit_ln1_fm(C, qbs, tag, sv, ss, stt)
        if DEBUG_STAGE < 2:
            continue
        bf_ready = {c: [] for c in range(NHC)}
        if fwht:
            slotmap = list(range(NHC))
            slots = {i: hidc(i) for i in range(NHC)}
            slots[64], slots[65] = spare[0], spare[1]
            free = [64, 65]
            plan = []
            for bit in range(6):
                step = 1 << bit
                for c0 in range(NHC):
                    if c0 & step:
                        continue
                    rdy = min(c0 | (2 * step - 1), NHC - 1)
                    plan.append((rdy, bit, c0))
            plan.sort(key=lambda t: (t[0], t[1], t[2]))
            for rdy, bit, c0 in plan:
                c1 = c0 | (1 << bit)
                a_s, b_s = slotmap[c0], slotmap[c1]
                d0, d1 = free
                bf_ready[rdy].append((slots[d0], slots[a_s], slots[b_s], True))
                bf_ready[rdy].append((slots[d1], slots[a_s], slots[b_s], False))
                slotmap[c0], slotmap[c1] = d0, d1
                free = [a_s, b_s]
            getc = lambda c, _sl=slots, _sm=list(slotmap): _sl[_sm[c]]
        else:
            getc = hidc
        def up_load(c):
            slot = ps_up.k % 2
            ps_up.load([(guslot(slot), gu_src[c * 128:(c + 1) * 128, :])])
        up_load(0)
        up_load(1)
        C.wait("tensor", sv)
        prod_marks = []
        h128_marks = []
        for c in range(NHC):
            slot = c % 2
            bg, bu = C.ps[0 + 2 * slot], C.ps[1 + 2 * slot]
            if c + 2 < NHC:
                up_load(c + 2)
            ps_up.use()
            if len(prod_marks) >= 2:
                C.wait("tensor", sv, prod_marks[-2])
            for dc in range(16):
                C.op("tensor", lambda e, _b=bg,
                     _w=guslot(slot)[:, dc * 128:(dc + 1) * 128],
                     _x=xnf[:, dc * 512:(dc + 1) * 512], _st=(dc == 0), _sp=(dc == 15):
                     e.matmul(_b.ap(), _w, _x, start=_st, stop=_sp))
            for dc in range(16):
                C.op("tensor", lambda e, _b=bu,
                     _w=guslot(slot)[:, 2048 + dc * 128: 2048 + (dc + 1) * 128],
                     _x=xnf[:, dc * 512:(dc + 1) * 512], _st=(dc == 0), _sp=(dc == 15):
                     e.matmul(_b.ap(), _w, _x, start=_st, stop=_sp),
                     ps_up.fr if dc == 15 else None)
            ps_up.done_attached()
            sil = af32v(C, 512 * (c % 2), 512)
            C.wait("scalar", ps_up.fr)
            if c >= 2:
                C.wait("scalar", sv, prod_marks[-2] if len(prod_marks) >= 2 else 0)
            C.op("scalar", lambda e, _s=sil, _b=bg: e.activation(_s, _b.ap(), AF.Silu), ss,
                 after=(ss,))
            C.wait("vector", ss)
            C.op("vector", lambda e, _o=hidc(c), _s=sil, _b=bu:
                 e.tensor_tensor(_o, _s, _b.ap(), ALU.mult), sv)
            prod_marks.append(C.sems[sv.name].n)
            if fwht:
                hb = C.ps[4 + (c % 2)]
                if c >= 2:
                    C.wait("tensor", sv, h128_marks[c - 2])
                C.wait("tensor", sv, prod_marks[-1])
                C.op("tensor", lambda e, _b=hb, _h=hidc(c):
                     e.matmul(_b.ap(), sb["h128"][:, :], _h, start=True,
                              stop=True), stt)
                C.wait("vector", stt)
                C.op("vector", lambda e, _o=hidc(c), _b=hb:
                     e.tensor_copy(_o, _b.ap()), sv, after=(sv,))
                h128_marks.append(C.sems[sv.name].n)
                for _o, _a, _bb, _add in bf_ready[c]:
                    if _add:
                        C.op("vector", lambda e, _oo=_o, _aa=_a, _b2=_bb:
                             e.tensor_add(_oo, _aa, _b2), sv, after=(sv,))
                    else:
                        C.op("vector", lambda e, _oo=_o, _aa=_a, _b2=_bb:
                             e.tensor_sub(_oo, _aa, _b2), sv)
        if DEBUG_STAGE < 3:
            continue
        if DEBUG_STAGE < 4:
            continue
        zt = xnf
        st = C.sb["stats"]
        C.wait("gpsimd", ps_up.fr)
        for half in range(2):
            for c in range(NHC):
                ps_dn.load([(dsl(c), down_src[(half * 64 + c) * 128:
                                              (half * 64 + c + 1) * 128, :])])
            C.wait("tensor", sv)
            for c in range(NHC):
                ps_dn.use()
                for qc in range(4):
                    for sl in range(2):
                        bank = C.ps[qc * 2 + sl]
                        C.op("tensor", lambda e, _b=bank,
                             _l=getc(c)[:, qc * 128:(qc + 1) * 128],
                             _r=dsl(c)[:, sl * 512:(sl + 1) * 512],
                             _st=(c == 0), _sp=(c == NHC - 1):
                             e.matmul(_b.ap(), _l, _r, start=_st, stop=_sp),
                             ps_dn.fr if (qc == 3 and sl == 1) else None)
                ps_dn.done_attached()
            C.wait("scalar", ps_dn.fr)
            for qc in range(4):
                for sl in range(2):
                    bank = C.ps[qc * 2 + sl]
                    dout0 = half * 1024 + sl * 512
                    zsl = zt[:, qc * 2048 + dout0: qc * 2048 + dout0 + 512]
                    pz = st[:, 16 + qc * 4 + half * 2 + sl: 17 + qc * 4 + half * 2 + sl]
                    C.op("scalar", lambda e, _z=zsl, _b=bank, _p=pz:
                         e.activation(_z, _b.ap(), AF.Identity, accum_out=_p), ss,
                         after=(ss,))
            if half == 0:
                C.wait("tensor", ss)
        if DEBUG_STAGE < 5:
            continue
        C.wait("vector", ss)
        for qi, qb in enumerate(qbs):
            zqb = zt[:, qi * 2048:(qi + 1) * 2048]
            sx = st[:, 1:2]
            ssq = st[:, 0:1]
            C.op("vector", lambda e, _o=sx, _a=st[:, 16 + qi * 4:17 + qi * 4],
                 _b=st[:, 17 + qi * 4:18 + qi * 4]: e.tensor_add(_o, _a, _b), sv, after=(sv,))
            C.op("vector", lambda e, _o=sx, _b=st[:, 18 + qi * 4:19 + qi * 4]:
                 e.tensor_add(_o, _o, _b), sv, after=(sv,))
            C.op("vector", lambda e, _o=sx, _b=st[:, 19 + qi * 4:20 + qi * 4]:
                 e.tensor_add(_o, _o, _b), sv, after=(sv,))
            sqd = af32v(C, 1024, D)
            C.wait("scalar", sv)
            C.op("scalar", lambda e, _s=sqd, _z=zqb, _p=ssq:
                 e.activation(_s, _z, AF.Square, accum_out=_p), ss, after=(ss,))
            C.wait("vector", ss)
            emit_ln(C, zqb, sv, ss, res_add=xr[:, qb * D:(qb + 1) * D],
                    presq=ssq, presum=sx)
        C.wait("scalar", sv)
        C.wait("tensor", sv)
        C.wait("gpsimd", sv)
        C.wait("sync", sv)


def pieces_of(jp):
    if jp <= 3:
        return [(jp * 128, 512 - jp * 128), (512, 512)]
    return [(jp * 128, 1024 - jp * 128)]


POFS = [0, 1024, 1920, 2688, 3328, 3840, 4224, 4480]


def emit_attn_head(C, P, h, h8, prev_tail=None):
    abf = C.sb["abf"]
    mtri, onesc, onesr, bfs = C.sb["mtri"], C.sb["onesc"], C.sb["onesr"], C.sb["bfs"]
    sv, ss, stt, spe, cc = P["sv"], P["ss"], P["stt"], P["spe"], P["cc"]
    ps_k, ps_v = P["ps_k"], P["ps_v"]
    k_gath, v_gath, kgr = P["k_gath"], P["v_gath"], P["kgr"]
    negM = P["negM"]
    first_sync_kv = P["fskv"]
    pm = P.setdefault("pm", {})
    srow = C.sb["af32"][0:1, 0:1024]
    KP, VP, PP, QG, AO = 8192, 10240, 12288, 0, 20992
    if first_sync_kv[0]:
        C.wait("sync", cc, 2)
    ps_k.load([(abf[:, KP + (ps_k.k % 2) * 1024: KP + (ps_k.k % 2 + 1) * 1024],
                k_gath.ap()[0 * kgr + h * 128: 0 * kgr + (h + 1) * 128, :])])
    if first_sync_kv[0]:
        C.wait("sync", cc, 3)
        first_sync_kv[0] = False
    ps_v.load([(abf[:, VP + (ps_v.k % 2) * 1024: VP + (ps_v.k % 2 + 1) * 1024]
                .rearrange("p (jb d) -> p jb d", jb=8),
                v_gath.ap()[0 * 1024:1 * 1024, h * 128:(h + 1) * 128]
                .rearrange("(jb p) d -> p jb d", p=128))])
    ktok = [ps_k.pending[-1]]
    vtok = [ps_v.pending[-1]]
    for r in range(1, 8):
        ps_k.load([(abf[:, KP + (ps_k.k % 2) * 1024: KP + (ps_k.k % 2 + 1) * 1024],
                    k_gath.ap()[r * kgr + h * 128: r * kgr + (h + 1) * 128, :])])
        ps_v.load([(abf[:, VP + (ps_v.k % 2) * 1024: VP + (ps_v.k % 2 + 1) * 1024]
                    .rearrange("p (jb d) -> p jb d", jb=8),
                    v_gath.ap()[r * 1024:(r + 1) * 1024, h * 128:(h + 1) * 128]
                    .rearrange("(jb p) d -> p jb d", p=128))])
        ktok.append(ps_k.pending[-1])
        vtok.append(ps_v.pending[-1])
    exp_marks = {}
    mask_marks = {}
    sum_first = {6: True, 7: True}
    pv_first = {0: True, 1: True}

    def block(b):
        r, jp = divmod(b, 8)
        if jp == 0:
            C.wait("tensor", ps_k.ld[ktok[r][0]], ktok[r][1])
            C.wait("tensor", ps_v.ld[vtok[r][0]], vtok[r][1])
        pid = 1 + (b % 2)
        pr = C.pr[pid]
        if pid in pm:
            C.wait("tensor", ss, pm[pid])
        kslot = abf[:, KP + (r % 2) * 1024: KP + (r % 2 + 1) * 1024]
        sc_mark = None
        for pi, (off, w) in enumerate(pieces_of(jp)):
            C.op("tensor", lambda e, _pr=pr, _o=off, _w=w,
                 _k=kslot[:, jp * 128:(jp + 1) * 128],
                 _q=abf[:, QG + h8 * 1024 + off: QG + h8 * 1024 + off + w]:
                 e.matmul(_pr.ap()[:, _o:_o + _w], _k, _q, start=True, stop=True),
                 stt)
            sc_mark = stt.n
        wq = 1024 - jp * 128
        C.wait("scalar", stt, sc_mark)
        C.op("scalar", lambda e, _p=abf[:, PP + POFS[jp]: PP + POFS[jp] + wq],
             _i=pr.ap()[:, jp * 128:1024], _n=negM():
             e.activation(_p, _i, AF.Exp, bias=_n), ss)
        if jp == 7:
            ps_k.done("scalar")
        exp_marks[b] = ss.n
        pm[pid] = ss.n
        C.wait("vector", ss, exp_marks[b])
        C.op("vector", lambda e, _p=abf[:, PP + POFS[jp]: PP + POFS[jp] + 128],
             _m=mtri[:, r * 128:(r + 1) * 128]:
             e.tensor_mul(_p, _p, _m), sv, after=(sv,))
        mask_marks[b] = sv.n

    def consume(b):
        r, jp = divmod(b, 8)
        vslot = abf[:, VP + (r % 2) * 1024: VP + (r % 2 + 1) * 1024]
        C.wait("tensor", ss, exp_marks[b])
        C.wait("tensor", sv, mask_marks[b])
        for pi, (off, w) in enumerate(pieces_of(jp)):
            bank = 6 if off < 512 else 7
            bstop = (r == 7) and ((jp == 3 and off < 512) or
                                  (jp == 7 and off >= 512))
            pp = abf[:, PP + POFS[jp] + (off - jp * 128):
                     PP + POFS[jp] + (off - jp * 128) + w]
            C.op("tensor", lambda e, _o=off, _w=w, _p=pp,
                 _st=sum_first[bank], _sp=bstop:
                 e.matmul(C.pr[3].ap()[0:1, _o:_o + _w], onesc[:, :], _p,
                          start=_st, stop=_sp),
                 spe if bstop else None)
            sum_first[bank] = False
            obank = 0 if off < 512 else 1
            lastpv = (jp == 7 and pi == len(pieces_of(jp)) - 1)
            C.op("tensor", lambda e, _o=off, _w=w, _p=pp,
                 _v=vslot[:, jp * 128:(jp + 1) * 128],
                 _st=pv_first[obank], _sp=bstop:
                 e.matmul(C.pr[0].ap()[:, _o:_o + _w], _v, _p, start=_st, stop=_sp),
                 ps_v.fr if lastpv else None)
            pv_first[obank] = False
        if jp == 7:
            ps_v.done_attached()

    if prev_tail is None:
        C.wait("tensor", sv)
    block(0)
    block(1)
    if prev_tail is not None:
        prev_tail()
    C.wait("tensor", sv)
    for b in range(2, 64):
        block(b)
        consume(b - 2)
    consume(62)
    consume(63)

    def tail():
        C.wait("vector", spe)
        C.wait("vector", ps_v.fr)
        C.op("vector", lambda e: e.tensor_copy(srow, C.pr[3].ap()[0:1, 0:1024]),
             sv, after=(sv,))
        C.op("vector", lambda e: e.tensor_scalar_add(srow, srow, 1e-30),
             sv, after=(sv,))
        C.op("vector", lambda e: e.tensor_copy(bfs[0:1, 1024:2048], srow),
             sv, after=(sv,))
        C.wait("tensor", sv)
        for i in range(2):
            C.op("tensor", lambda e, _i=i: e.matmul(
                C.ps[6 + _i].ap(), onesr[:, :],
                bfs[0:1, 1024 + _i * 512: 1024 + (_i + 1) * 512],
                start=True, stop=True), stt)
        C.wait("vector", stt)
        rb = af32v(C, 4096, 1024)
        C.op("vector", lambda e: e.tensor_copy(rb, C.pr[3].ap()), sv, after=(sv,))
        C.op("vector", lambda e: e.reciprocal(rb, rb), sv, after=(sv,))
        C.op("vector", lambda e, _h=h: e.tensor_mul(
            abf[:, AO + _h * 1024: AO + (_h + 1) * 1024],
            C.pr[0].ap(), rb), sv, after=(sv,))

    return tail


def emit_outproj_ln2(C, tag, wsrc, NH, bo1, prev_fr, pan=False, cast_gate=None):
    sb = C.sb
    xr, abf, bfs = sb["xr"], sb["abf"], sb["bfs"]
    onesr = sb["onesr"]
    af = sb["af32"]
    borow = af[0:1, 2048:4096]
    WQP, AO = 16896, 20992
    sv = C.sem(tag + "v")
    ss = C.sem(tag + "s")
    stt = C.sem(tag + "t")
    spe = C.sem(tag + "pe")
    sbi = C.sem(tag + "bi")
    C.wait("sync", sv)
    C.wait("sync", ss)
    C.dma("sync", borow, bo1, sbi)
    C.wait("vector", sbi)
    C.op("vector", lambda e: e.tensor_copy(bfs[0:1, 0:2048], borow), sv, after=(sv,))
    C.wait("tensor", sv)
    _bcm2 = []
    for i in range(4):
        if i >= 2:
            C.wait("tensor", sv, _bcm2[i - 2])
        C.op("tensor", lambda e, _i=i: e.matmul(
            C.ps[4 + (_i % 2)].ap(), onesr[:, :], bfs[0:1, _i * 512: (_i + 1) * 512],
            start=True, stop=True), stt)
        C.wait("vector", stt)
        C.op("vector", lambda e, _i=i: e.tensor_copy(
            af32v(C, 2048 + _i * 512, 512), C.ps[4 + (_i % 2)].ap()), sv)
        _bcm2.append(C.sems[tag + "v"].n)
    bo_b = af32v(C, 2048, 2048)

    ps_wo = PanelStream(C, tag + "wo", nbuf=4)
    zt = abf[:, 0:16384]
    C.wait("gpsimd", prev_fr)

    def wo_load(sl, h):
        dst = abf[:, WQP + (ps_wo.k % 8) * 512: WQP + (ps_wo.k % 8) * 512 + 512]
        if pan:
            r0 = (sl * NH + h) * 128
            ps_wo.load([(dst, wsrc[r0:r0 + 128, :])])
        else:
            ps_wo.load([(dst, wsrc[h * 128:(h + 1) * 128, sl * 512:(sl + 1) * 512])])

    wo_load(0, 0)
    wo_load(0, 1)
    C.wait("tensor", sv)
    zev_marks = {}
    for sl in range(4):
        for h in range(NH):
            nxt = sl * NH + h + 2
            if nxt < 4 * NH:
                wo_load(nxt // NH, nxt % NH)
            ps_wo.use()
            if sl > 0 and h == 0:
                C.wait("tensor", sv, zev_marks[sl - 1])
            for qb in range(8):
                wsl = abf[:, WQP + (ps_wo.uk % 8) * 512: WQP + (ps_wo.uk % 8) * 512 + 512]
                carry = None
                if qb == 7:
                    carry = ps_wo.fr
                elif h == NH - 1:
                    carry = spe
                C.op("tensor", lambda e, _b=C.ps[qb],
                     _l=abf[:, AO + h * 1024 + qb * 128: AO + h * 1024 + qb * 128 + 128],
                     _r=wsl, _st=(h == 0), _sp=(h == NH - 1):
                     e.matmul(_b.ap(), _l, _r, start=_st, stop=_sp), carry)
            ps_wo.done_attached()
        C.wait("vector", spe)
        C.wait("vector", ps_wo.fr)
        for qb in range(8):
            C.op("vector", lambda e, _z=zt[:, qb * 2048 + sl * 512: qb * 2048 + sl * 512 + 512],
                 _b=C.ps[qb], _bo=bo_b[:, sl * 512:(sl + 1) * 512]:
                 e.tensor_add(_z, _b.ap(), _bo), sv, after=(sv,))
        zev_marks[sl] = sv.n

    C.wait("scalar", sv)
    emit_ln_pipe(C, [(zt[:, qb * 2048:(qb + 1) * 2048],
                      xr[:, qb * D:(qb + 1) * D]) for qb in range(8)],
                 sv, ss, bufs=[af32v(C, 0, D), af32v(C, 2048, D)])


def emit_attn0(C, tag, wqT, wkT, wvT, woT, bq2, bk2, bv1, bo1, mtri_in,
               cg,
               k_loc, v_loc, k_gath, v_gath, knm_loc, knm_gath):
    sb = C.sb
    xr, xnf, abf = sb["xr"], sb["xnf"], sb["abf"]
    st, pbias, bfs, bcol = sb["stats"], sb["pbias"], sb["bfs"], sb["bcol"]
    af = sb["af32"]
    bvrow = af[0:1, 2048:4096]
    nrow = af[0:1, 2048:2560]
    knmrow = af[0:1, 2560:2576]
    kgrow = af[0:1, 2688:2816]
    srow = af[0:1, 0:1024]
    borow = af[0:1, 2048:4096]
    mtri, onesc, onesr = sb["mtri"], sb["onesc"], sb["onesr"]
    QG, KP, VP, PP, WQP, AO = 0, 8192, 10240, 12288, 16896, 20992
    sv = C.sem(tag + "v")
    ss = C.sem(tag + "s")
    stt = C.sem(tag + "t")
    spe = C.sem(tag + "pe")
    cc = C.sem(tag + "cc")
    ksm = [C.sem(tag + f"ks{i}") for i in range(4)]
    ksn = C.sem(tag + "ksn")
    vsm = [C.sem(tag + f"vs{i}") for i in range(2)]
    sdm = C.sem(tag + "dm")
    sbi = C.sem(tag + "bi")

    def wk_slot(i):
        return abf[:, WQP + (i % 2) * 2048: WQP + (i % 2 + 1) * 2048]

    def wv_slot(i):
        return abf[:, WQP + (i % 4) * 512: WQP + (i % 4 + 1) * 512]

    emit_ln1_fm(C, list(range(8)), tag, sv, ss, stt)
    C.dma("sync", mtri[:, :], mtri_in, sdm)

    fq = af32v(C, 0, 128)
    fk = af32v(C, 128, 128)
    C.wait("sync", ss)
    C.wait("sync", sv)
    C.dma("sync", fq[0:16, :], bq2, sbi)
    C.dma("sync", fk[0:16, :], bk2, sbi)
    C.dma("sync", bvrow, bv1, sbi)
    C.wait("vector", sbi)
    C.wait("vector", ss)
    C.op("vector", lambda e: e.tensor_scalar_mul(fq[0:16, :], fq[0:16, :], SCALE0),
         sv, after=(sv,))
    C.op("vector", lambda e: e.tensor_copy(bfs[0:16, 0:128], fq[0:16, :]), sv, after=(sv,))
    C.op("vector", lambda e: e.tensor_scalar_mul(fk[0:16, :], fk[0:16, :], SCALE0),
         sv, after=(sv,))
    C.op("vector", lambda e: e.tensor_copy(bfs[0:16, 128:256], fk[0:16, :]), sv, after=(sv,))
    C.wait("tensor", sv)
    bqp = C.ps[6].bitcast(BF16)[:, 0:16]
    bkp = C.ps[7].bitcast(BF16)[:, 0:16]
    C.op("tensor", lambda e: e.transpose(bqp, bfs[0:16, 0:128], sb["ident"][0:16, 0:16]), stt)
    C.op("tensor", lambda e: e.transpose(bkp, bfs[0:16, 128:256], sb["ident"][0:16, 0:16]), stt)
    C.wait("vector", stt)
    C.op("vector", lambda e: e.tensor_copy(bcol[:, 0:16], bqp), sv, after=(sv,))
    C.op("vector", lambda e: e.tensor_copy(bcol[:, 16:32], bkp), sv, after=(sv,))
    C.wait("vector", stt)
    C.op("vector", lambda e: e.tensor_copy(bfs[0:1, 0:2048], bvrow), sv, after=(sv,))
    C.wait("tensor", sv)
    _bcm = []
    for i in range(4):
        if i >= 2:
            C.wait("tensor", sv, _bcm[i - 2])
        C.op("tensor", lambda e, _i=i: e.matmul(
            C.ps[4 + (_i % 2)].ap(), onesr[:, :],
            bfs[0:1, _i * 512: (_i + 1) * 512],
            start=True, stop=True), stt)
        C.wait("vector", stt)
        C.op("vector", lambda e, _i=i: e.tensor_copy(
            af32v(C, _i * 512, 512), C.ps[4 + (_i % 2)].ap()), sv)
        _bcm.append(C.sems[tag + "v"].n)
    bv_b = af32v(C, 0, 2048)

    ps_wk = PanelStream(C, tag + "wk")
    ksq = abf[:, PP:PP + 1024]

    def kw_load(h, wsrc):
        ps_wk.load([(wk_slot(ps_wk.k), wsrc[h * 128:(h + 1) * 128, :])])

    kw_load(0, wkT)
    kw_load(1, wkT)
    C.wait("tensor", sv)
    ev_marks = {}
    mm_marks = {}
    sq_marks = {}
    kst_marks = {}
    ncp_marks = {}
    for h in range(16):
        slot = h % 2
        if h + 2 < 16:
            kw_load(h + 2, wkT)
        ps_wk.use()
        if h >= 2:
            C.wait("tensor", ss, ev_marks[h - 2])
        for nh in range(2):
            for dc in range(16):
                last = (nh == 1 and dc == 15)
                C.op("tensor", lambda e, _b=C.ps[2 * slot + nh],
                     _w=wk_slot(slot)[:, dc * 128:(dc + 1) * 128],
                     _x=xnf[:, dc * 1024 + nh * 512: dc * 1024 + nh * 512 + 512],
                     _st=(dc == 0), _sp=(dc == 15):
                     e.matmul(_b.ap(), _w, _x, start=_st, stop=_sp),
                     (ps_wk.fr if last else (stt if dc == 15 else None)))
        ps_wk.done_attached()
        kp = abf[:, KP + (h % 4) * 1024: KP + (h % 4 + 1) * 1024]
        C.wait("scalar", stt)
        C.wait("scalar", ps_wk.fr)
        if h >= 4:
            _sm, _vv = kst_marks[h - 4]
            C.wait("scalar", _sm, _vv)
        for nh in range(2):
            C.op("scalar", lambda e, _o=kp[:, nh * 512:(nh + 1) * 512],
                 _b=C.ps[2 * slot + nh], _bias=bcol[:, 16 + h:17 + h]:
                 e.activation(_o, _b.ap(), AF.Identity, bias=_bias, scale=SCALE0),
                 ss, after=(ss,))
        ev_marks[h] = ss.n
        C.wait("sync", ss, ev_marks[h])
        C.dma("sync", k_loc.ap()[h * 128:(h + 1) * 128, :], kp, ksm[h % 4])
        kst_marks[h] = (ksm[h % 4], ksm[h % 4].n)
        if h >= 1:
            C.wait("scalar", stt, mm_marks[h - 1])
        C.op("scalar", lambda e, _kp=kp: e.activation(ksq, _kp, AF.Square), ss, after=(ss,))
        sq_marks[h] = ss.n
        C.wait("tensor", ss, sq_marks[h])
        if h >= 2:
            C.wait("tensor", sv, ncp_marks[h - 2])
        for nh in range(2):
            C.op("tensor", lambda e, _b=C.ps[4 + slot],
                 _r=ksq[:, nh * 512:(nh + 1) * 512], _st=(nh == 0), _sp=(nh == 1):
                 e.matmul(_b.ap()[0:1, 0:512], onesc[:, :], _r,
                          start=_st, stop=_sp), stt)
        mm_marks[h] = stt.n
        C.wait("vector", stt, mm_marks[h])
        C.op("vector", lambda e, _s=slot: e.tensor_copy(nrow, C.ps[4 + _s].ap()[0:1, 0:512]),
             sv, after=(sv,))
        ncp_marks[h] = sv.n
        C.op("vector", lambda e: e.reduce_max(st[0:1, 33:34], nrow, AX.X),
             sv, after=(sv,))
        if h == 0:
            C.op("vector", lambda e: e.tensor_copy(st[0:1, 32:33], st[0:1, 33:34]),
                 sv, after=(sv,))
        else:
            C.op("vector", lambda e: e.tensor_max(st[0:1, 32:33], st[0:1, 32:33], st[0:1, 33:34]),
                 sv, after=(sv,))
    C.op("vector", lambda e: e.memset(knmrow, 0.0), sv, after=(sv,))
    C.op("vector", lambda e: e.tensor_copy(knmrow[0:1, 0:1], st[0:1, 32:33]), sv, after=(sv,))
    C.wait("sync", sv)
    C.dma("sync", knm_loc.ap(), knmrow, ksn)
    ps_wv = PanelStream(C, tag + "wv", nbuf=4)
    C.wait("gpsimd", ps_wk.fr)

    def vw_load(hg, dc):
        r0 = (hg * 16 + dc) * 128
        ps_wv.load([(wv_slot(ps_wv.k), wvT[r0:r0 + 128, :])])

    vst_marks = {}
    vev_marks = {}
    for pre in range(4):
        vw_load(0, pre)
    for _k in ksm:
        C.wait("gpsimd", _k)
    C.wait("gpsimd", ksn)
    C.op("gpsimd", lambda e: e.collective_compute(
        "AllGather", ALU.bypass, replica_groups=[list(range(NCORES))],
        ins=[knm_loc.ap()], outs=[knm_gath.ap()]), cc)
    C.op("gpsimd", lambda e: e.collective_compute(
        "AllGather", ALU.bypass, replica_groups=[list(range(NCORES))],
        ins=[k_loc.ap()], outs=[k_gath.ap()]), cc)
    C.wait("tensor", sv)
    for hg in range(4):
        for dc in range(16):
            if not (hg == 0 and dc < 4):
                pass
            nxt = hg * 16 + dc + 4
            if nxt < 64:
                vw_load((nxt) // 16, (nxt) % 16)
            ps_wv.use()
            if hg > 0 and dc == 0:
                C.wait("tensor", sv, vev_marks[hg - 1])
            for qb in range(8):
                last = (qb == 7)
                C.op("tensor", lambda e, _b=C.ps[qb],
                     _l=xnf[:, dc * 1024 + qb * 128: dc * 1024 + qb * 128 + 128],
                     _r=wv_slot(ps_wv.uk), _st=(dc == 0), _sp=(dc == 15):
                     e.matmul(_b.ap(), _l, _r, start=_st, stop=_sp),
                     (ps_wv.fr if last else (stt if (dc == 15 and qb == 7) else None)))
            ps_wv.done_attached()
        C.wait("vector", ps_wv.fr)
        strip = abf[:, QG + (hg % 2) * 4096: QG + (hg % 2) * 4096 + 4096]
        if hg >= 2:
            _sm, _vv = vst_marks[hg - 2]
            C.wait("vector", _sm, _vv)
        for qb in range(8):
            C.op("vector", lambda e, _o=strip[:, qb * 512:(qb + 1) * 512],
                 _b=C.ps[qb], _bias=bv_b[:, hg * 512:(hg + 1) * 512]:
                 e.tensor_add(_o, _b.ap(), _bias), sv, after=(sv,))
        C.wait("sync", sv)
        C.dma("sync",
              v_loc.ap()[:, hg * 512:(hg + 1) * 512]
              .rearrange("(qb p) d -> p qb d", p=128),
              strip.rearrange("p (qb d) -> p qb d", qb=8), vsm[hg % 2])
        vst_marks[hg] = (vsm[hg % 2], vsm[hg % 2].n)
        vev_marks[hg] = sv.n
    ps_wq = PanelStream(C, tag + "wq")
    ps_k = PanelStream(C, tag + "gk", dma_eng="sync")
    ps_v = PanelStream(C, tag + "gv", dma_eng="sync")
    qsq = abf[:, PP:PP + 1024]
    C.wait("gpsimd", ps_wv.fr)

    def qw_load(h):
        ps_wq.load([(wk_slot(ps_wq.k), wqT[h * 128:(h + 1) * 128, :])])

    qw_load(0)
    qw_load(1)
    for _v in vsm:
        C.wait("gpsimd", _v)
    C.op("gpsimd", lambda e: e.collective_compute(
        "AllGather", ALU.bypass, replica_groups=[list(range(NCORES))],
        ins=[v_loc.ap()], outs=[v_gath.ap()]), cc)
    C.wait("scalar", vsm[0])
    C.wait("scalar", vsm[1])
    first_sync_kv = [True]

    ao_marks = {}
    for g in range(2):
        if g == 1:
            qw_load(g * 8)
            qw_load(g * 8 + 1)
        C.wait("tensor", sv)
        C.wait("scalar", stt)
        C.wait("scalar", spe)
        C.wait("scalar", C.sems[tag + "gv_f"])
        qev = {}
        qmm = {}
        qncp = {}
        for h8 in range(8):
            h = g * 8 + h8
            slot = h8 % 2
            if h8 + 2 < 8:
                qw_load(h + 2)
            ps_wq.use()
            if h8 >= 2:
                C.wait("tensor", ss, qev[h8 - 2])
            for nh in range(2):
                for dc in range(16):
                    last = (nh == 1 and dc == 15)
                    C.op("tensor", lambda e, _b=C.ps[2 * slot + nh],
                         _w=wk_slot(ps_wq.uk)[:, dc * 128:(dc + 1) * 128],
                         _x=xnf[:, dc * 1024 + nh * 512: dc * 1024 + nh * 512 + 512],
                         _st=(dc == 0), _sp=(dc == 15):
                         e.matmul(_b.ap(), _w, _x, start=_st, stop=_sp),
                         (ps_wq.fr if last else (stt if dc == 15 else None)))
            ps_wq.done_attached()
            C.wait("scalar", stt)
            C.wait("scalar", ps_wq.fr)
            for nh in range(2):
                C.op("scalar", lambda e,
                     _o=abf[:, QG + h8 * 1024 + nh * 512: QG + h8 * 1024 + nh * 512 + 512],
                     _b=C.ps[2 * slot + nh], _bias=bcol[:, h:h + 1]:
                     e.activation(_o, _b.ap(), AF.Identity, bias=_bias, scale=SCALE0),
                     ss, after=(ss,))
            qev[h8] = ss.n
            if h8 >= 1:
                C.wait("scalar", stt, qmm[h8 - 1])
            C.op("scalar", lambda e, _q=abf[:, QG + h8 * 1024: QG + (h8 + 1) * 1024]:
                 e.activation(qsq, _q, AF.Square), ss, after=(ss,))
            C.wait("tensor", ss)
            if h8 >= 2:
                C.wait("tensor", sv, qncp[h8 - 2])
            for nh in range(2):
                C.op("tensor", lambda e, _b=C.ps[4 + slot],
                     _r=qsq[:, nh * 512:(nh + 1) * 512], _st=(nh == 0), _sp=(nh == 1):
                     e.matmul(_b.ap()[0:1, 0:512], onesc[:, :], _r,
                              start=_st, stop=_sp), stt)
            qmm[h8] = stt.n
            C.wait("vector", stt, qmm[h8])
            C.op("vector", lambda e, _b=C.ps[4 + slot]:
                 e.tensor_copy(nrow, _b.ap()[0:1, 0:512]), sv, after=(sv,))
            qncp[h8] = sv.n
            C.op("vector", lambda e: e.reduce_max(st[0:1, 36:37], nrow, AX.X),
                 sv, after=(sv,))
            if h8 == 0:
                C.op("vector", lambda e, _g=g: e.tensor_copy(st[0:1, 34 + _g:35 + _g],
                                                             st[0:1, 36:37]), sv, after=(sv,))
            else:
                C.op("vector", lambda e, _g=g: e.tensor_max(st[0:1, 34 + _g:35 + _g],
                                                            st[0:1, 34 + _g:35 + _g],
                                                            st[0:1, 36:37]), sv, after=(sv,))
        if g == 0:
            C.wait("sync", cc, 1)
            C.dma("sync", kgrow.rearrange("a (r k) -> a r k", r=8),
                  knm_gath.ap().rearrange("(a r) k -> a r k", a=1), sdm)
        C.wait("vector", sdm)
        C.op("vector", lambda e: e.reduce_max(st[0:1, 37:38], kgrow, AX.X),
             sv, after=(sv,))
        C.op("vector", lambda e, _g=g: e.tensor_tensor(st[0:1, 38:39], st[0:1, 34 + _g:35 + _g],
                                                       st[0:1, 37:38], ALU.mult), sv, after=(sv,))
        C.wait("scalar", sv)
        C.op("scalar", lambda e: e.sqrt(st[0:1, 39:40], st[0:1, 38:39]), ss)
        C.wait("vector", ss)
        C.op("vector", lambda e: e.tensor_scalar_mul(st[0:1, 39:40], st[0:1, 39:40], -1.0),
             sv, after=(sv,))
        C.op("vector", lambda e: e.tensor_copy(bfs[0:1, 0:1], st[0:1, 39:40]),
             sv, after=(sv,))
        C.wait("tensor", sv)
        C.op("tensor", lambda e: e.matmul(C.ps[6].ap()[:, 0:1], onesr[:, :],
                                          bfs[0:1, 0:1], start=True, stop=True), stt)
        C.wait("vector", stt)
        C.op("vector", lambda e, _g=g: e.tensor_copy(pbias[:, 40 + _g:41 + _g],
                                                     C.ps[6].ap()[:, 0:1]), sv)
        negM = lambda _g=g: pbias[:, 40 + _g:41 + _g]

        P = dict(sv=sv, ss=ss, stt=stt, spe=spe, cc=cc, ps_k=ps_k, ps_v=ps_v,
                 k_gath=k_gath, v_gath=v_gath, kgr=2048, negM=negM, fskv=first_sync_kv)
        for h8 in range(8):
            h = g * 8 + h8
            P["tail"] = emit_attn_head(C, P, h, h8, P.get("tail"))
        P["tail"]()
        P["tail"] = None

    emit_outproj_ln2(C, tag, woT, 16, bo1, ps_wq.fr, pan=True)


def emit_attn1(C, tag, wq, wk, wv, wo, bq1, bk1, bv1, bo1, mtri_in, rope_in,
               k_loc, v_loc, k_gath, v_gath, knm_loc, knm_gath):
    sb = C.sb
    xr, xnf, abf = sb["xr"], sb["xnf"], sb["abf"]
    st, pbias, bfs = sb["stats"], sb["pbias"], sb["bfs"]
    mtri, onesc, onesr, rope = sb["mtri"], sb["onesc"], sb["onesr"], sb["rope"]
    af = sb["af32"]
    QG, KP, VP, PP, WQP, AO = 0, 8192, 10240, 12288, 16896, 20992
    sv = C.sem(tag + "v")
    ss = C.sem(tag + "s")
    stt = C.sem(tag + "t")
    spe = C.sem(tag + "pe")
    cc = C.sem(tag + "cc")
    ksn = C.sem(tag + "ksn")
    ksm = [C.sem(tag + "ks0"), C.sem(tag + "ks1")]
    vsm = [C.sem(tag + f"vs{i}") for i in range(4)]
    sdm = C.sem(tag + "dm")
    sbi = C.sem(tag + "bi")

    emit_ln1_fm(C, list(range(8)), tag, sv, ss, stt)
    C.dma("sync", mtri[:, :], mtri_in, sdm)
    sdr = C.sem(tag + "dr")
    C.dma("gpsimd", rope[:, :], rope_in, sdr)
    C.wait("vector", sdr)

    rowq = af[0:1, 2048:4096]
    C.wait("sync", ss)
    C.wait("sync", sv)
    C.dma("sync", rowq, bq1, sbi)
    C.dma("sync", af[0:1, 4096:4608], bk1, sbi)
    C.dma("sync", af[0:1, 4608:5120], bv1, sbi)
    C.wait("vector", sbi)
    beff = af[0:1, 1536:2048]
    for h in range(4):
        C.op("vector", lambda e, _h=h: e.tensor_add(
            beff[0:1, _h * 128:(_h + 1) * 128],
            rowq[0:1, (4 * _h) * 128:(4 * _h) * 128 + 128],
            rowq[0:1, (4 * _h + 1) * 128:(4 * _h + 1) * 128 + 128]), sv, after=(sv,))
        for g in (2, 3):
            C.op("vector", lambda e, _h=h, _g=g: e.tensor_add(
                beff[0:1, _h * 128:(_h + 1) * 128],
                beff[0:1, _h * 128:(_h + 1) * 128],
                rowq[0:1, (4 * _h + _g) * 128:(4 * _h + _g) * 128 + 128]), sv, after=(sv,))
    C.op("vector", lambda e: e.tensor_copy(bfs[0:1, 0:512], beff), sv, after=(sv,))
    C.op("vector", lambda e: e.tensor_copy(bfs[0:1, 512:1024], af[0:1, 4096:4608]),
         sv, after=(sv,))
    C.op("vector", lambda e: e.tensor_copy(bfs[0:1, 1024:1536], af[0:1, 4608:5120]),
         sv, after=(sv,))
    C.wait("tensor", sv)
    _bm = []
    for i in range(3):
        if i >= 2:
            C.wait("tensor", sv, _bm[i - 2])
        C.op("tensor", lambda e, _i=i: e.matmul(
            C.ps[4 + (_i % 2)].ap(), onesr[:, :], bfs[0:1, _i * 512:(_i + 1) * 512],
            start=True, stop=True), stt)
        C.wait("vector", stt)
        C.op("vector", lambda e, _i=i: e.tensor_copy(
            af32v(C, _i * 512, 512), C.ps[4 + (_i % 2)].ap()), sv)
        _bm.append(C.sems[tag + "v"].n)
    bqe_b = af32v(C, 0, 512)
    bk_b = af32v(C, 512, 512)
    bv_b = af32v(C, 1024, 512)

    ps_w = PanelStream(C, tag + "wp")

    def rope_pass(which, wsrc, ncols, bias_b, tab_off, dst_tm):
        for pre in range(2):
            ps_w.load([(abf[:, WQP + (ps_w.k % 2) * 2048: WQP + (ps_w.k % 2) * 2048 + ncols],
                        wsrc[pre * 128:(pre + 1) * 128, :])])
        for dc in range(16):
            if dc + 2 < 16:
                ps_w.load([(abf[:, WQP + (ps_w.k % 2) * 2048: WQP + (ps_w.k % 2) * 2048 + ncols],
                            wsrc[(dc + 2) * 128:(dc + 3) * 128, :])])
            ps_w.use()
            wsl = abf[:, WQP + (ps_w.uk % 2) * 2048: WQP + (ps_w.uk % 2) * 2048 + ncols]
            if dc == 0:
                C.wait("tensor", sv)
            for jb in range(8):
                xt = xnf[:, dc * 1024 + jb * 128: dc * 1024 + jb * 128 + 128]
                if which == "q":
                    for hg in range(16):
                        hh, gg = divmod(hg, 4)
                        last = (jb == 7 and hg == 15)
                        C.op("tensor", lambda e, _b=C.ps[jb], _x=xt,
                             _r=wsl[:, (4 * hh + gg) * 128:(4 * hh + gg) * 128 + 128],
                             _o=hh * 128, _st=(dc == 0 and hg == 0),
                             _sp=(dc == 15 and hg == 15):
                             e.matmul(_b.ap()[:, _o:_o + 128], _x, _r,
                                      start=_st, stop=_sp),
                             ps_w.fr if last else None)
                else:
                    last = (jb == 7)
                    C.op("tensor", lambda e, _b=C.ps[jb], _x=xt, _r=wsl[:, 0:512],
                         _st=(dc == 0), _sp=(dc == 15):
                         e.matmul(_b.ap(), _x, _r, start=_st, stop=_sp),
                         ps_w.fr if last else None)
            ps_w.done_attached()
        C.wait("vector", ps_w.fr)
        qt = af[:, 1536:1664]
        ta = af[:, 1664:1728]
        tb = af[:, 1728:1792]
        x1f = qt.rearrange("p (k t) -> p t k", t=2)[:, 0:1, :]
        x2f = qt.rearrange("p (k t) -> p t k", t=2)[:, 1:2, :]
        taf = ta.rearrange("p (o k) -> p o k", o=1)
        tbf = tb.rearrange("p (o k) -> p o k", o=1)
        for jb in range(8):
            if which == "v":
                vp = abf[:, VP + (jb % 4) * 512: VP + (jb % 4) * 512 + 512]
                if jb >= 4:
                    _sm, _vv = vstm[jb - 4]
                    C.wait("vector", _sm, _vv)
                C.op("vector", lambda e, _o=vp, _b=C.ps[jb], _bb=bias_b:
                     e.tensor_add(_o, _b.ap(), _bb), sv, after=(sv,))
                C.wait("sync", sv)
                C.dma("sync", v_loc.ap()[jb * 128:(jb + 1) * 128, :], vp, vsm[jb % 4])
                vstm[jb] = (vsm[jb % 4], vsm[jb % 4].n)
                continue
            for h in range(4):
                pslice = C.ps[jb].ap()[:, h * 128:(h + 1) * 128]
                cqf = rope[:, jb * 256 + tab_off: jb * 256 + tab_off + 64] \
                    .rearrange("p (o k) -> p o k", o=1)
                sqf = rope[:, jb * 256 + tab_off + 64: jb * 256 + tab_off + 128] \
                    .rearrange("p (o k) -> p o k", o=1)
                dst = dst_tm[:, jb * 512 + h * 128: jb * 512 + h * 128 + 128]
                C.op("vector", lambda e, _o=qt, _p=pslice,
                     _bb=bias_b[:, h * 128:(h + 1) * 128]:
                     e.tensor_add(_o, _p, _bb), sv, after=(sv,))
                C.op("vector", lambda e, _a=taf, _x=x1f, _c=cqf:
                     e.tensor_mul(_a, _x, _c), sv, after=(sv,))
                C.op("vector", lambda e, _b2=tbf, _x=x2f, _s=sqf:
                     e.tensor_mul(_b2, _x, _s), sv, after=(sv,))
                C.op("vector", lambda e, _o=dst[:, 0:64], _a=ta, _b2=tb:
                     e.tensor_sub(_o, _a, _b2), sv, after=(sv,))
                C.op("vector", lambda e, _a=taf, _x=x1f, _s=sqf:
                     e.tensor_mul(_a, _x, _s), sv, after=(sv,))
                C.op("vector", lambda e, _b2=tbf, _x=x2f, _c=cqf:
                     e.tensor_mul(_b2, _x, _c), sv, after=(sv,))
                C.op("vector", lambda e, _o=dst[:, 64:128], _a=ta, _b2=tb:
                     e.tensor_add(_o, _a, _b2), sv, after=(sv,))

    q_tm = abf[:, QG + 4096: QG + 8192]
    vstm = {}
    rope_pass("q", wq, 2048, bqe_b, 0, q_tm)
    C.wait("tensor", sv)
    tmark = []
    for jb in range(8):
        for h in range(4):
            t = jb * 4 + h
            bank = C.ps[6 + (t % 2)]
            pv = bank.bitcast(BF16)[:, 0:128]
            if len(tmark) >= 2:
                C.wait("tensor", sv, tmark[-2])
            C.op("tensor", lambda e, _p=pv,
                 _i=q_tm[:, jb * 512 + h * 128: jb * 512 + h * 128 + 128]:
                 e.transpose(_p, _i, sb["ident"][:, :]), stt)
            C.wait("vector", stt)
            C.op("vector", lambda e, _o=abf[:, QG + h * 1024 + jb * 128:
                                            QG + h * 1024 + jb * 128 + 128], _p=pv:
                 e.tensor_copy(_o, _p), sv)
            tmark.append(C.sems[tag + "v"].n)
    dmp = af[:, 1536:1664]
    C.wait("scalar", sv)
    for jb in range(8):
        for h in range(4):
            C.op("scalar", lambda e, _q=q_tm[:, jb * 512 + h * 128: jb * 512 + h * 128 + 128],
                 _p=st[:, 16 + jb * 4 + h:17 + jb * 4 + h]:
                 e.activation(dmp, _q, AF.Square, accum_out=_p), ss, after=(ss,))
    C.wait("vector", ss)
    C.op("vector", lambda e: e.reduce_max(st[:, 48:49], st[:, 16:48], AX.X), sv, after=(sv,))
    C.op("vector", lambda e: e.tensor_copy(sb["bcol"][:, 32:33], st[:, 48:49]), sv, after=(sv,))
    C.wait("tensor", sv)
    qnp = C.ps[6].bitcast(BF16)[0:1, 0:128]
    C.op("tensor", lambda e: e.transpose(qnp, sb["bcol"][:, 32:33], sb["ident"][:, :]), stt)
    C.wait("vector", stt)
    C.op("vector", lambda e: e.tensor_copy(af[0:1, 1536:1664], qnp), sv, after=(sv,))
    C.op("vector", lambda e: e.reduce_max(st[0:1, 34:35], af[0:1, 1536:1664], AX.X),
         sv, after=(sv,))

    C.wait("scalar", stt)
    rope_pass("k", wk, 512, bk_b, 128, q_tm)
    C.wait("tensor", sv)
    ksq = abf[:, PP:PP + 1024]
    tmark = []
    kst = {}
    for h in range(4):
        kp = abf[:, KP + (h % 2) * 1024: KP + (h % 2) * 1024 + 1024]
        if h >= 2:
            _sm, _vv = kst[h - 2]
            C.wait("vector", _sm, _vv)
        for jb in range(8):
            t = h * 8 + jb
            bank = C.ps[6 + (t % 2)]
            pv = bank.bitcast(BF16)[:, 0:128]
            if len(tmark) >= 2:
                C.wait("tensor", sv, tmark[-2])
            C.op("tensor", lambda e, _p=pv,
                 _i=q_tm[:, jb * 512 + h * 128: jb * 512 + h * 128 + 128]:
                 e.transpose(_p, _i, sb["ident"][:, :]), stt)
            C.wait("vector", stt)
            C.op("vector", lambda e, _o=kp[:, jb * 128:(jb + 1) * 128], _p=pv:
                 e.tensor_copy(_o, _p), sv)
            tmark.append(C.sems[tag + "v"].n)
        C.wait("sync", sv)
        C.dma("sync", k_loc.ap()[h * 128:(h + 1) * 128, :], kp, ksm[h % 2])
        kst[h] = (ksm[h % 2], ksm[h % 2].n)
        C.wait("scalar", sv)
        C.op("scalar", lambda e, _kp=kp: e.activation(ksq, _kp, AF.Square), ss, after=(ss,))
        C.wait("tensor", ss)
        for nh in range(2):
            C.op("tensor", lambda e, _r=ksq[:, nh * 512:(nh + 1) * 512],
                 _st=(nh == 0), _sp=(nh == 1):
                 e.matmul(C.ps[5].ap()[0:1, 0:512], onesc[:, :], _r,
                          start=_st, stop=_sp), stt)
        C.wait("vector", stt)
        C.op("vector", lambda e: e.tensor_copy(af[0:1, 2048:2560], C.ps[5].ap()[0:1, 0:512]),
             sv, after=(sv,))
        C.op("vector", lambda e: e.reduce_max(st[0:1, 33:34], af[0:1, 2048:2560], AX.X),
             sv, after=(sv,))
        if h == 0:
            C.op("vector", lambda e: e.tensor_copy(st[0:1, 32:33], st[0:1, 33:34]),
                 sv, after=(sv,))
        else:
            C.op("vector", lambda e: e.tensor_max(st[0:1, 32:33], st[0:1, 32:33],
                                                  st[0:1, 33:34]), sv, after=(sv,))
    knmrow = af[0:1, 2560:2576]
    C.op("vector", lambda e: e.memset(knmrow, 0.0), sv, after=(sv,))
    C.op("vector", lambda e: e.tensor_copy(knmrow[0:1, 0:1], st[0:1, 32:33]), sv, after=(sv,))
    C.wait("sync", sv)
    C.dma("sync", knm_loc.ap(), knmrow, ksn)
    C.wait("gpsimd", ksm[0])
    C.wait("gpsimd", ksm[1])
    C.wait("gpsimd", ksn)
    C.op("gpsimd", lambda e: e.collective_compute(
        "AllGather", ALU.bypass, replica_groups=[list(range(NCORES))],
        ins=[knm_loc.ap()], outs=[knm_gath.ap()]), cc)
    C.op("gpsimd", lambda e: e.collective_compute(
        "AllGather", ALU.bypass, replica_groups=[list(range(NCORES))],
        ins=[k_loc.ap()], outs=[k_gath.ap()]), cc)

    C.wait("scalar", stt)
    rope_pass("v", wv, 512, bv_b, 0, None)
    for _v in vsm:
        C.wait("gpsimd", _v)
    C.op("gpsimd", lambda e: e.collective_compute(
        "AllGather", ALU.bypass, replica_groups=[list(range(NCORES))],
        ins=[v_loc.ap()], outs=[v_gath.ap()]), cc)

    C.wait("sync", cc, 1)
    kgrow = af[0:1, 2688:2816]
    C.dma("sync", kgrow.rearrange("a (r k) -> a r k", r=8),
          knm_gath.ap().rearrange("(a r) k -> a r k", a=1), sdm)
    C.wait("vector", sdm)
    C.op("vector", lambda e: e.reduce_max(st[0:1, 37:38], kgrow, AX.X), sv, after=(sv,))
    C.op("vector", lambda e: e.tensor_tensor(st[0:1, 38:39], st[0:1, 34:35],
                                             st[0:1, 37:38], ALU.mult), sv, after=(sv,))
    C.wait("scalar", sv)
    C.op("scalar", lambda e: e.sqrt(st[0:1, 39:40], st[0:1, 38:39]), ss)
    C.wait("vector", ss)
    C.op("vector", lambda e: e.tensor_scalar_mul(st[0:1, 39:40], st[0:1, 39:40], -1.0),
         sv, after=(sv,))
    C.op("vector", lambda e: e.tensor_copy(bfs[0:1, 0:1], st[0:1, 39:40]), sv, after=(sv,))
    C.wait("tensor", sv)
    C.op("tensor", lambda e: e.matmul(C.ps[6].ap()[:, 0:1], onesr[:, :],
                                      bfs[0:1, 0:1], start=True, stop=True), stt)
    C.wait("vector", stt)
    C.op("vector", lambda e: e.tensor_copy(pbias[:, 42:43], C.ps[6].ap()[:, 0:1]), sv)
    negM = lambda: pbias[:, 42:43]

    ps_k = PanelStream(C, tag + "gk", dma_eng="sync")
    ps_v = PanelStream(C, tag + "gv", dma_eng="sync")
    first_sync_kv = [True]
    P = dict(sv=sv, ss=ss, stt=stt, spe=spe, cc=cc, ps_k=ps_k, ps_v=ps_v,
             k_gath=k_gath, v_gath=v_gath, kgr=512, negM=negM, fskv=first_sync_kv)
    C.wait("tensor", sv)
    C.wait("scalar", stt)
    tail = None
    for h in range(4):
        tail = emit_attn_head(C, P, h, h, tail)
    tail()

    emit_outproj_ln2(C, tag, wo, 4, bo1, ps_w.fr)


def shard_stream(X):
    xb = np.ascontiguousarray(X.reshape(64, 128, D).astype(np.float32))
    return [np.ascontiguousarray(xb[c::8].reshape(TPC, D)) for c in range(NCORES)]


def unshard_stream(outs):
    xb = np.zeros((64, 128, D), np.float32)
    for c in range(NCORES):
        xb[c::8] = np.asarray(outs[c]).reshape(8, 128, D)
    return xb.reshape(1, SEQ, D)


def host_consts():
    ident = np.eye(128, dtype=np.float32)
    h = np.array([[1.0]])
    while h.shape[0] < 128:
        h = np.block([[h, h], [h, -h]])
    h128 = (h / np.sqrt(8192.0)).astype(np.float32)
    return ident, h128


_PROG_CACHE = {}


def build_ffn_program(tag, HD):
    nc = bass.Bass("TRN2", target_bir_lowering=False)
    x_in = nc.dram_tensor("x_in", [TPC, D], F32, kind="ExternalInput").ap()
    gu_w = nc.dram_tensor("gu_w", [HD, 2 * D], F32, kind="ExternalInput").ap()
    d_w = nc.dram_tensor("d_w", [2 * HD, 1024], F32, kind="ExternalInput").ap()
    h128_in = nc.dram_tensor("h128_in", [128, 128], F32, kind="ExternalInput").ap()
    ident_in = nc.dram_tensor("ident_in", [128, 128], F32, kind="ExternalInput").ap()
    x_out = nc.dram_tensor("x_out", [TPC, D], F32, kind="ExternalOutput").ap()

    stack = ExitStack()
    C = Ctx(nc, stack)
    alloc_common(C)
    sdi = C.sem("din")
    sdg = C.sem("ding")
    sdo = C.sem("dout")
    C.dma("sync", C.sb["xr"][:, :].rearrange("p (qb f) -> p qb f", qb=NQB),
          x_in.rearrange("(qb p) f -> p qb f", p=128), sdi)
    C.dma("gpsimd", C.sb["ident"][:, :], ident_in, sdg)
    C.dma("gpsimd", C.sb["h128"][:, :], h128_in, sdg)
    for e in ENGS:
        C.wait(e, sdi)
        C.wait(e, sdg)
    emit_ffn(C, tag, gu_w, d_w, fwht=(tag == "f0"))
    C.wait("sync", C.sems[tag + "v"])
    C.dma("sync", x_out.rearrange("(qb p) f -> p qb f", p=128),
          C.sb["xr"][:, :].rearrange("p (qb f) -> p qb f", qb=NQB), sdo)
    C.wait("sync", sdo)
    C.finalize()
    stack.close()
    return nc


def build_attn0_program(tag):
    nc = bass.Bass("TRN2", target_bir_lowering=False)
    x_in = nc.dram_tensor("x_in", [TPC, D], F32, kind="ExternalInput").ap()
    wq = nc.dram_tensor("wq", [D, D], F32, kind="ExternalInput").ap()
    wk = nc.dram_tensor("wk", [D, D], F32, kind="ExternalInput").ap()
    wv = nc.dram_tensor("wv", [8192, 512], F32, kind="ExternalInput").ap()
    wo = nc.dram_tensor("wo", [8192, 512], F32, kind="ExternalInput").ap()
    bq2 = nc.dram_tensor("bq2", [16, 128], F32, kind="ExternalInput").ap()
    bk2 = nc.dram_tensor("bk2", [16, 128], F32, kind="ExternalInput").ap()
    bv1 = nc.dram_tensor("bv1", [1, D], F32, kind="ExternalInput").ap()
    bo1 = nc.dram_tensor("bo1", [1, D], F32, kind="ExternalInput").ap()
    mtri_in = nc.dram_tensor("mtri_in", [128, 1024], BF16, kind="ExternalInput").ap()
    ident_in = nc.dram_tensor("ident_in", [128, 128], F32, kind="ExternalInput").ap()
    onesc_in = nc.dram_tensor("onesc_in", [128, 1], F32, kind="ExternalInput").ap()
    onesr_in = nc.dram_tensor("onesr_in", [1, 128], F32, kind="ExternalInput").ap()
    x_out = nc.dram_tensor("x_out", [TPC, D], F32, kind="ExternalOutput").ap()

    k_loc = nc.dram_tensor("k_loc", [D, TPC], BF16, kind="Internal")
    v_loc = nc.dram_tensor("v_loc", [TPC, D], BF16, kind="Internal")
    k_gath = nc.dram_tensor("k_gath", [8 * D, TPC], BF16, kind="Internal")
    v_gath = nc.dram_tensor("v_gath", [SEQ, D], BF16, kind="Internal")
    knm_loc = nc.dram_tensor("knm_loc", [1, 16], F32, kind="Internal")
    knm_gath = nc.dram_tensor("knm_gath", [8, 16], F32, kind="Internal")

    stack = ExitStack()
    C = Ctx(nc, stack)
    alloc_common(C)
    sdi = C.sem("din")
    sdg = C.sem("ding")
    sdo = C.sem("dout")
    C.dma("sync", C.sb["xr"][:, :].rearrange("p (qb f) -> p qb f", qb=NQB),
          x_in.rearrange("(qb p) f -> p qb f", p=128), sdi)
    C.dma("gpsimd", C.sb["ident"][:, :], ident_in, sdg)
    C.dma("gpsimd", C.sb["onesc"][:, :], onesc_in, sdg)
    C.dma("gpsimd", C.sb["onesr"][:, :], onesr_in, sdg)
    for e in ENGS:
        C.wait(e, sdi)
        C.wait(e, sdg)
    emit_attn0(C, tag, wq, wk, wv, wo, bq2, bk2, bv1, bo1, mtri_in,
               None, k_loc, v_loc, k_gath, v_gath, knm_loc, knm_gath)
    C.wait("sync", C.sems[tag + "v"])
    C.dma("sync", x_out.rearrange("(qb p) f -> p qb f", p=128),
          C.sb["xr"][:, :].rearrange("p (qb f) -> p qb f", qb=NQB), sdo)
    C.wait("sync", sdo)
    C.finalize()
    stack.close()
    return nc


def pack_gu(wT):
    return np.ascontiguousarray(
        wT.reshape(16, 128, 64, 128).transpose(2, 1, 0, 3).reshape(8192, 2048))


def pack_gu2(gT, uT):
    g = pack_gu(gT).reshape(64, 128, 2048)
    u = pack_gu(uT).reshape(64, 128, 2048)
    return np.ascontiguousarray(np.concatenate([g, u], axis=2).reshape(8192, 4096))


def pack_dn(wT):
    return np.ascontiguousarray(
        wT.reshape(64, 128, 2, 1024).transpose(2, 0, 1, 3).reshape(16384, 1024))


def pack_kq(wT):
    return np.ascontiguousarray(
        wT.reshape(16, 128, 16, 128).transpose(2, 1, 0, 3).reshape(2048, 2048))


def pack_v(wT):
    return np.ascontiguousarray(
        wT.reshape(16, 128, 4, 512).transpose(2, 0, 1, 3).reshape(8192, 512))


def pack_wo(wT, NH=16):
    return np.ascontiguousarray(
        wT.reshape(NH, 128, 4, 512).transpose(2, 0, 1, 3).reshape(4 * NH * 128, 512))


def host_mtri(c):
    m = np.zeros((128, 1024), np.float32)
    for r in range(8):
        if r < c:
            m[:, r * 128:(r + 1) * 128] = 1.0
        elif r == c:
            s = np.arange(128)[:, None]
            q = np.arange(128)[None, :]
            m[:, r * 128:(r + 1) * 128] = (q >= s).astype(np.float32)
    return m.astype(ml_dtypes.bfloat16)


def run_attn0_launch(x_shards, Wq, bq, Wk, bk, Wv, bv, Wo, bo):
    key = ("a0",)
    if key not in _PROG_CACHE:
        _PROG_CACHE[key] = build_attn0_program("a0")
    nc = _PROG_CACHE[key]
    ident, _ = host_consts()
    onesc = np.ones((128, 1), np.float32)
    onesr = np.ones((1, 128), np.float32)
    wqT = pack_kq(Wq.T)
    wkT = pack_kq(Wk.T)
    wvT = pack_v(Wv.T)
    woT = pack_wo(Wo.T, 16)
    in_maps = []
    for c in range(NCORES):
        in_maps.append(dict(
            x_in=x_shards[c], wq=wqT, wk=wkT, wv=wvT, wo=woT,
            bq2=np.ascontiguousarray(bq.reshape(16, 128)),
            bk2=np.ascontiguousarray(bk.reshape(16, 128)),
            bv1=np.ascontiguousarray(bv.reshape(1, D)),
            bo1=np.ascontiguousarray(bo.reshape(1, D)),
            mtri_in=host_mtri(c), ident_in=ident,
            onesc_in=onesc, onesr_in=onesr))
    res = run_bass_kernel_spmd(nc, in_maps, core_ids=list(range(NCORES)))
    return [res.results[c]["x_out"] for c in range(NCORES)]


def build_attn1_program(tag):
    nc = bass.Bass("TRN2", target_bir_lowering=False)
    x_in = nc.dram_tensor("x_in", [TPC, D], F32, kind="ExternalInput").ap()
    wq = nc.dram_tensor("wq", [D, D], F32, kind="ExternalInput").ap()
    wk = nc.dram_tensor("wk", [D, 512], F32, kind="ExternalInput").ap()
    wv = nc.dram_tensor("wv", [D, 512], F32, kind="ExternalInput").ap()
    wo = nc.dram_tensor("wo", [512, D], F32, kind="ExternalInput").ap()
    bq1 = nc.dram_tensor("bq1", [1, D], F32, kind="ExternalInput").ap()
    bk1 = nc.dram_tensor("bk1", [1, 512], F32, kind="ExternalInput").ap()
    bv1 = nc.dram_tensor("bv1", [1, 512], F32, kind="ExternalInput").ap()
    bo1 = nc.dram_tensor("bo1", [1, D], F32, kind="ExternalInput").ap()
    mtri_in = nc.dram_tensor("mtri_in", [128, 1024], BF16, kind="ExternalInput").ap()
    rope_in = nc.dram_tensor("rope_in", [128, 2048], F32, kind="ExternalInput").ap()
    ident_in = nc.dram_tensor("ident_in", [128, 128], F32, kind="ExternalInput").ap()
    onesc_in = nc.dram_tensor("onesc_in", [128, 1], F32, kind="ExternalInput").ap()
    onesr_in = nc.dram_tensor("onesr_in", [1, 128], F32, kind="ExternalInput").ap()
    x_out = nc.dram_tensor("x_out", [TPC, D], F32, kind="ExternalOutput").ap()

    k_loc = nc.dram_tensor("k_loc", [512, TPC], BF16, kind="Internal")
    v_loc = nc.dram_tensor("v_loc", [TPC, 512], BF16, kind="Internal")
    k_gath = nc.dram_tensor("k_gath", [8 * 512, TPC], BF16, kind="Internal")
    v_gath = nc.dram_tensor("v_gath", [SEQ, 512], BF16, kind="Internal")
    knm_loc = nc.dram_tensor("knm_loc", [1, 16], F32, kind="Internal")
    knm_gath = nc.dram_tensor("knm_gath", [8, 16], F32, kind="Internal")

    stack = ExitStack()
    C = Ctx(nc, stack)
    alloc_common(C)
    sdi = C.sem("din")
    sdg = C.sem("ding")
    sdo = C.sem("dout")
    C.dma("sync", C.sb["xr"][:, :].rearrange("p (qb f) -> p qb f", qb=NQB),
          x_in.rearrange("(qb p) f -> p qb f", p=128), sdi)
    C.dma("gpsimd", C.sb["ident"][:, :], ident_in, sdg)
    C.dma("gpsimd", C.sb["onesc"][:, :], onesc_in, sdg)
    C.dma("gpsimd", C.sb["onesr"][:, :], onesr_in, sdg)
    for e in ENGS:
        C.wait(e, sdi)
        C.wait(e, sdg)
    emit_attn1(C, tag, wq, wk, wv, wo, bq1, bk1, bv1, bo1, mtri_in, rope_in,
               k_loc, v_loc, k_gath, v_gath, knm_loc, knm_gath)
    C.wait("sync", C.sems[tag + "v"])
    C.dma("sync", x_out.rearrange("(qb p) f -> p qb f", p=128),
          C.sb["xr"][:, :].rearrange("p (qb f) -> p qb f", qb=NQB), sdo)
    C.wait("sync", sdo)
    C.finalize()
    stack.close()
    return nc


def host_rope(c):
    out = np.zeros((128, 2048), np.float32)
    inv = 1.0 / 10000.0 ** (np.arange(64) * 2.0 / 128)
    s1 = 1.0 / np.sqrt(128.0)
    for jb in range(8):
        pos = (8 * jb + c) * 128 + np.arange(128)
        emb = pos[:, None] * inv[None, :]
        out[:, jb * 256 + 0:jb * 256 + 64] = np.cos(emb) * s1
        out[:, jb * 256 + 64:jb * 256 + 128] = np.sin(emb) * s1
        out[:, jb * 256 + 128:jb * 256 + 192] = np.cos(emb)
        out[:, jb * 256 + 192:jb * 256 + 256] = np.sin(emb)
    return out


def run_attn1_launch(x_shards, Wq, bq, Wk, bk, Wv, bv, Wo, bo):
    key = ("a1",)
    if key not in _PROG_CACHE:
        _PROG_CACHE[key] = build_attn1_program("a1")
    nc = _PROG_CACHE[key]
    ident, _ = host_consts()
    onesc = np.ones((128, 1), np.float32)
    onesr = np.ones((1, 128), np.float32)
    in_maps = []
    for c in range(NCORES):
        in_maps.append(dict(
            x_in=x_shards[c], wq=np.ascontiguousarray(Wq),
            wk=np.ascontiguousarray(Wk), wv=np.ascontiguousarray(Wv),
            wo=np.ascontiguousarray(Wo),
            bq1=np.ascontiguousarray(bq.reshape(1, D)),
            bk1=np.ascontiguousarray(bk.reshape(1, 512)),
            bv1=np.ascontiguousarray(bv.reshape(1, 512)),
            bo1=np.ascontiguousarray(bo.reshape(1, D)),
            mtri_in=host_mtri(c), rope_in=host_rope(c), ident_in=ident,
            onesc_in=onesc, onesr_in=onesr))
    res = run_bass_kernel_spmd(nc, in_maps, core_ids=list(range(NCORES)))
    return [res.results[c]["x_out"] for c in range(NCORES)]


def run_ffn_launch(tag, x_shards, g_w, u_w, d_w):
    HD = g_w.shape[1]
    key = (tag, HD)
    if key not in _PROG_CACHE:
        _PROG_CACHE[key] = build_ffn_program(tag, HD)
    nc = _PROG_CACHE[key]
    ident, h128 = host_consts()
    in_maps = [dict(x_in=x_shards[c], g_w=g_w, u_w=u_w, d_w=d_w,
                    h128_in=h128, ident_in=ident) for c in range(NCORES)]
    res = run_bass_kernel_spmd(nc, in_maps, core_ids=list(range(NCORES)))
    return [res.results[c]["x_out"] for c in range(NCORES)]


def phase_barrier(C, tags):
    for e in ENGS:
        for name, s in list(C.sems.items()):
            if any(name.startswith(t) for t in tags) and s.n > 0:
                C.wait(e, s)


def build_fused_program():
    nc = bass.Bass("TRN2", target_bir_lowering=False)
    T = lambda n, sh: nc.dram_tensor(n, sh, F32, kind="ExternalInput").ap()
    x_in = T("x_in", [TPC, D])
    a0wq, a0wk = T("a0wq", [D, D]), T("a0wk", [D, D])
    a0wv, a0wo = T("a0wv", [8192, 512]), T("a0wo", [8192, 512])
    a0bq, a0bk = T("a0bq", [16, 128]), T("a0bk", [16, 128])
    a0bv, a0bo = T("a0bv", [1, D]), T("a0bo", [1, D])
    f0gu = T("f0gu", [8192, 4096])
    f0d = T("f0d", [16384, 1024])
    a1wq = T("a1wq", [D, D])
    a1wk, a1wv = T("a1wk", [D, 512]), T("a1wv", [D, 512])
    a1wo = T("a1wo", [512, D])
    a1bq, a1bk = T("a1bq", [1, D]), T("a1bk", [1, 512])
    a1bv, a1bo = T("a1bv", [1, 512]), T("a1bo", [1, D])
    f1gu = T("f1gu", [8192, 4096])
    f1d = T("f1d", [16384, 1024])
    mtri_in = nc.dram_tensor("mtri_in", [128, 1024], BF16, kind="ExternalInput").ap()
    rope_in = T("rope_in", [128, 2048])
    ident_in = T("ident_in", [128, 128])
    onesc_in = T("onesc_in", [128, 1])
    onesr_in = T("onesr_in", [1, 128])
    h128_in = T("h128_in", [128, 128])
    x_out = nc.dram_tensor("x_out", [TPC, D], F32, kind="ExternalOutput").ap()

    k_loc0 = nc.dram_tensor("k_loc0", [D, TPC], BF16, kind="Internal")
    v_loc0 = nc.dram_tensor("v_loc0", [TPC, D], BF16, kind="Internal")
    k_gath0 = nc.dram_tensor("k_gath0", [8 * D, TPC], BF16, kind="Internal")
    v_gath0 = nc.dram_tensor("v_gath0", [SEQ, D], BF16, kind="Internal")
    knm_loc0 = nc.dram_tensor("knm_loc0", [1, 16], F32, kind="Internal")
    knm_gath0 = nc.dram_tensor("knm_gath0", [8, 16], F32, kind="Internal")
    k_loc1 = nc.dram_tensor("k_loc1", [512, TPC], BF16, kind="Internal")
    v_loc1 = nc.dram_tensor("v_loc1", [TPC, 512], BF16, kind="Internal")
    k_gath1 = nc.dram_tensor("k_gath1", [8 * 512, TPC], BF16, kind="Internal")
    v_gath1 = nc.dram_tensor("v_gath1", [SEQ, 512], BF16, kind="Internal")
    knm_loc1 = nc.dram_tensor("knm_loc1", [1, 16], F32, kind="Internal")
    knm_gath1 = nc.dram_tensor("knm_gath1", [8, 16], F32, kind="Internal")

    stack = ExitStack()
    C = Ctx(nc, stack)
    alloc_common(C)
    sdi = C.sem("din")
    sdg = C.sem("ding")
    sdo = C.sem("dout")
    C.dma("sync", C.sb["xr"][:, :].rearrange("p (qb f) -> p qb f", qb=NQB),
          x_in.rearrange("(qb p) f -> p qb f", p=128), sdi)
    C.dma("gpsimd", C.sb["ident"][:, :], ident_in, sdg)
    C.dma("gpsimd", C.sb["onesc"][:, :], onesc_in, sdg)
    C.dma("gpsimd", C.sb["onesr"][:, :], onesr_in, sdg)
    C.dma("gpsimd", C.sb["h128"][:, :], h128_in, sdg)
    for e in ENGS:
        C.wait(e, sdi)
        C.wait(e, sdg)
    emit_attn0(C, "a0", a0wq, a0wk, a0wv, a0wo, a0bq, a0bk, a0bv, a0bo, mtri_in,
               None, k_loc0, v_loc0, k_gath0, v_gath0, knm_loc0, knm_gath0)
    phase_barrier(C, ("a0",))
    emit_ffn(C, "f0", f0gu, f0d, fwht=True)
    phase_barrier(C, ("f0",))
    emit_attn1(C, "a1", a1wq, a1wk, a1wv, a1wo, a1bq, a1bk, a1bv, a1bo, mtri_in,
               rope_in, k_loc1, v_loc1, k_gath1, v_gath1, knm_loc1, knm_gath1)
    phase_barrier(C, ("a1",))
    emit_ffn(C, "f1", f1gu, f1d, fwht=False)
    phase_barrier(C, ("f1",))
    C.dma("sync", x_out.rearrange("(qb p) f -> p qb f", p=128),
          C.sb["xr"][:, :].rearrange("p (qb f) -> p qb f", qb=NQB), sdo)
    C.wait("sync", sdo)
    C.finalize()
    stack.close()
    return nc


def kernel_fused(g):
    key = ("fused",)
    if key not in _PROG_CACHE:
        _PROG_CACHE[key] = build_fused_program()
    nc = _PROG_CACHE[key]
    ident, h128 = host_consts()
    onesc = np.ones((128, 1), np.float32)
    onesr = np.ones((1, 128), np.float32)
    shards = shard_stream(g["L0_attn_X"])
    ct = np.ascontiguousarray
    shared = dict(
        a0wq=pack_kq(g["L0_attn_Wq"].T), a0wk=pack_kq(g["L0_attn_Wk"].T),
        a0wv=pack_v(g["L0_attn_Wv"].T), a0wo=pack_wo(g["L0_attn_Wo"].T, 16),
        a0bq=ct(g["L0_attn_bq"].reshape(16, 128)),
        a0bk=ct(g["L0_attn_bk"].reshape(16, 128)),
        a0bv=ct(g["L0_attn_bv"].reshape(1, D)),
        a0bo=ct(g["L0_attn_bo"].reshape(1, D)),
        f0gu=pack_gu2(g["L0_ffn_W1"].T, g["L0_ffn_W3"].T),
        f0d=pack_dn(g["L0_ffn_W2"].T),
        a1wq=ct(g["L1_attn_Wq"]), a1wk=ct(g["L1_attn_Wk"]),
        a1wv=ct(g["L1_attn_Wv"]), a1wo=ct(g["L1_attn_Wo"]),
        a1bq=ct(g["L1_attn_bq"].reshape(1, D)),
        a1bk=ct(g["L1_attn_bk"].reshape(1, 512)),
        a1bv=ct(g["L1_attn_bv"].reshape(1, 512)),
        a1bo=ct(g["L1_attn_bo"].reshape(1, D)),
        f1gu=pack_gu2(g["L1_ffn_w_gate_up"][:8192].T,
                      g["L1_ffn_w_gate_up"][8192:].T),
        f1d=pack_dn(g["L1_ffn_w_down"].T),
        ident_in=ident, onesc_in=onesc, onesr_in=onesr, h128_in=h128)
    in_maps = []
    for c in range(NCORES):
        m = dict(shared)
        m["x_in"] = shards[c]
        m["mtri_in"] = host_mtri(c)
        m["rope_in"] = host_rope(c)
        in_maps.append(m)
    res = run_bass_kernel_spmd(nc, in_maps, core_ids=list(range(NCORES)))
    return unshard_stream([res.results[c]["x_out"] for c in range(NCORES)])


FUSED = True


def kernel(**inputs):
    g = {k: np.ascontiguousarray(np.asarray(v, np.float32))
         for k, v in inputs.items()}
    if FUSED:
        return kernel_fused(g)
    shards = shard_stream(g["L0_attn_X"])
    shards = run_attn0_launch(shards, g["L0_attn_Wq"], g["L0_attn_bq"],
                              g["L0_attn_Wk"], g["L0_attn_bk"],
                              g["L0_attn_Wv"], g["L0_attn_bv"],
                              g["L0_attn_Wo"], g["L0_attn_bo"])
    shards = [np.ascontiguousarray(np.asarray(s)) for s in shards]
    shards = run_ffn_launch("f0", shards,
                            np.ascontiguousarray(g["L0_ffn_W1"].T),
                            np.ascontiguousarray(g["L0_ffn_W3"].T),
                            np.ascontiguousarray(g["L0_ffn_W2"].T))
    shards = [np.ascontiguousarray(np.asarray(s)) for s in shards]
    shards = run_attn1_launch(shards, g["L1_attn_Wq"], g["L1_attn_bq"],
                              g["L1_attn_Wk"], g["L1_attn_bk"],
                              g["L1_attn_Wv"], g["L1_attn_bv"],
                              g["L1_attn_Wo"], g["L1_attn_bo"])
    shards = [np.ascontiguousarray(np.asarray(s)) for s in shards]
    shards = run_ffn_launch("f1", shards,
                            np.ascontiguousarray(g["L1_ffn_w_gate_up"][:8192].T),
                            np.ascontiguousarray(g["L1_ffn_w_gate_up"][8192:].T),
                            np.ascontiguousarray(g["L1_ffn_w_down"].T))
    return unshard_stream(shards)


# version 12
# speedup vs baseline: 1.0121x; 1.0121x over previous
import numpy as np
import ml_dtypes
from contextlib import ExitStack

import concourse.bass as bass
import concourse.mybir as mybir
from concourse.bass_utils import run_bass_kernel_spmd

F32 = mybir.dt.float32
BF16 = mybir.dt.bfloat16
AF = mybir.ActivationFunctionType
ALU = mybir.AluOpType
AX = mybir.AxisListType

NCORES = 8
D = 2048
SEQ = 8192
TPC = 1024
NQB = 8
NH0 = 16
DH = 128
EPS = 1e-5
SCALE0 = float(1.0 / np.sqrt(np.sqrt(DH)))
ENGS = ("sync", "gpsimd", "tensor", "vector", "scalar")
DEBUG_STAGE = 99
DEBUG_TILES = 2


class SemT:
    __slots__ = ("h", "n", "name")

    def __init__(self, h, name):
        self.h = h
        self.n = 0
        self.name = name


class Ctx:
    def __init__(self, nc, stack):
        self.nc = nc
        self.stack = stack
        self.ops = {k: [] for k in ENGS}
        self.sems = {}
        self.ps = []
        self.sb = {}

    def sem(self, name):
        s = self.sems.get(name)
        if s is None:
            h = self.stack.enter_context(self.nc.semaphore(name))
            s = self.sems[name] = SemT(h, name)
        return s

    def emit(self, eng, f):
        self.ops[eng].append(f)

    def wait(self, eng, sem, val=None):
        v = sem.n if val is None else val
        if v > 0:
            self.emit(eng, lambda e, _s=sem, _v=v: e.wait_ge(_s.h, _v))

    def op(self, eng, f, sem=None, k=1, after=()):
        for a in after:
            self.wait(eng, a)
        if sem is not None:
            sem.n += k
            self.emit(eng, lambda e, _f=f, _s=sem, _k=k: _f(e).then_inc(_s.h, _k))
        else:
            self.emit(eng, f)

    def sig(self, eng, sem, k=1):
        sem.n += k
        self.emit(eng, lambda e, _s=sem, _k=k: e.nop().then_inc(_s.h, _k))

    def dma(self, eng, out, in_, sem, **kw):
        self.op(eng, lambda e, _o=out, _i=in_, _kw=kw: e.dma_start(out=_o, in_=_i, **_kw),
                sem, 16)

    def finalize(self):
        ops = self.ops
        with self.nc.Block() as block:
            @block.sync
            def _(e):
                for f in ops["sync"]:
                    f(e)

            @block.gpsimd
            def _(e):
                for f in ops["gpsimd"]:
                    f(e)

            @block.tensor
            def _(e):
                for f in ops["tensor"]:
                    f(e)

            @block.vector
            def _(e):
                for f in ops["vector"]:
                    f(e)

            @block.scalar
            def _(e):
                for f in ops["scalar"]:
                    f(e)


class PanelStream:

    def __init__(self, C, name, dma_eng="gpsimd", cons_eng="tensor", nbuf=2):
        self.C = C
        self.nbuf = nbuf
        self.ld = [C.sem(f"{name}_l{i}") for i in range(nbuf)]
        self.fr = C.sem(f"{name}_f")
        self.k = 0
        self.uk = 0
        self.pending = []
        self.dma_eng = dma_eng
        self.cons_eng = cons_eng

    def load(self, dmas):
        C = self.C
        k = self.k
        slot = k % self.nbuf
        if k >= self.nbuf:
            C.wait(self.dma_eng, self.fr, k - self.nbuf + 1)
        s = self.ld[slot]
        for d in dmas:
            dst, src = d[0], d[1]
            kw = d[2] if len(d) > 2 else {}
            C.dma(self.dma_eng, dst, src, s, **kw)
        self.pending.append((slot, s.n))
        self.k += 1

    def use(self, eng=None):
        slot, v = self.pending[0]
        self.C.wait(eng or self.cons_eng, self.ld[slot], v)

    def done(self, eng=None):
        self.C.sig(eng or self.cons_eng, self.fr, 1)
        self.uk += 1
        self.pending.pop(0)

    def done_attached(self):
        self.uk += 1
        self.pending.pop(0)


def alloc_common(C):
    nc = C.nc
    sb = C.sb
    sb["xr"] = nc.alloc_sbuf_tensor("xr", [128, NQB * D], F32)
    sb["xnf"] = nc.alloc_sbuf_tensor("xnf", [128, 16 * 1024], BF16)
    sb["abf"] = nc.alloc_sbuf_tensor("abf", [128, 37376], BF16)
    sb["bfs"] = nc.alloc_sbuf_tensor("bfs", [16, 2048], BF16)
    sb["bcol"] = nc.alloc_sbuf_tensor("bcol", [128, 64], BF16)
    sb["af32"] = nc.alloc_sbuf_tensor("af32", [128, 5120], F32)
    sb["mtri"] = nc.alloc_sbuf_tensor("mtri", [128, 8 * 128], BF16)
    sb["ident"] = nc.alloc_sbuf_tensor("ident", [128, 128], BF16)
    sb["onesc"] = nc.alloc_sbuf_tensor("onesc", [128, 1], BF16)
    sb["onesr"] = nc.alloc_sbuf_tensor("onesr", [1, 128], BF16)
    sb["pbias"] = nc.alloc_sbuf_tensor("pbias", [128, 96], F32)
    sb["stats"] = nc.alloc_sbuf_tensor("stats", [128, 64], F32)
    sb["h128"] = nc.alloc_sbuf_tensor("h128", [128, 128], BF16)
    sb["rope"] = nc.alloc_sbuf_tensor("rope", [128, 2048], BF16)
    C.pr = []
    for i in range(4):
        C.pr.append(nc.alloc_psum_tensor(f"pr{i}", [128, 1024], F32))
    for i in range(8):
        C.ps.append(PsView(C.pr[i // 2], (i % 2) * 512))


class PsView:
    def __init__(self, t, off):
        self.t = t
        self.off = off

    def ap(self):
        return self.t.ap()[:, self.off:self.off + 512]

    def bitcast(self, dt):
        return self.t.ap().bitcast(dt)[:, 2 * self.off: 2 * self.off + 1024]


def af32v(C, off, width):
    return C.sb["af32"][:, off:off + width]


def emit_ln(C, src_ap, sv, ss, out_ap=None, res_add=None, scratch_off=1024,
            presq=None, presum=None):
    st = C.sb["stats"]
    sq = af32v(C, scratch_off, D)
    ssq = st[:, 0:1]
    sx = st[:, 1:2]
    m = st[:, 2:3]
    var = st[:, 3:4]
    r = st[:, 4:5]
    b = st[:, 5:6]
    if presq is None:
        C.op("scalar", lambda e: e.activation(sq, src_ap, AF.Square, accum_out=ssq), ss,
             after=(ss,))
        C.op("vector", lambda e: e.reduce_sum(sx, src_ap, AX.X), sv, after=(sv,))
        C.wait("vector", ss)
    else:
        ssq, sx = presq, presum
    C.op("vector", lambda e: e.tensor_scalar_mul(m, sx, 1.0 / D), sv, after=(sv,))
    C.op("vector", lambda e: e.tensor_scalar_mul(var, ssq, 1.0 / D), sv, after=(sv,))
    C.op("vector", lambda e: e.tensor_tensor(b, m, m, ALU.mult), sv, after=(sv,))
    C.op("vector", lambda e: e.tensor_sub(var, var, b), sv, after=(sv,))
    C.op("vector", lambda e: e.tensor_scalar_add(var, var, EPS), sv, after=(sv,))
    C.wait("scalar", sv)
    C.op("scalar", lambda e: e.sqrt(var, var), ss)
    C.wait("vector", ss)
    C.op("vector", lambda e: e.reciprocal(r, var), sv, after=(sv,))
    C.op("vector", lambda e: e.tensor_tensor(b, m, r, ALU.mult), sv, after=(sv,))
    C.op("vector", lambda e: e.tensor_scalar_mul(b, b, -1.0), sv, after=(sv,))
    C.wait("scalar", sv)
    if res_add is None:
        C.op("scalar", lambda e: e.activation(out_ap, src_ap, AF.Identity,
                                              bias=b, scale=r), ss)
    else:
        C.op("scalar", lambda e: e.activation(sq, src_ap, AF.Identity,
                                              bias=b, scale=r), ss)
        C.wait("vector", ss)
        C.op("vector", lambda e: e.tensor_add(res_add, res_add, sq), sv, after=(sv,))


def emit_ln_pipe(C, items, sv, ss, bufs, stc0=16, sqdump=None, tr=None):
    st = C.sb["stats"]
    n = len(items)
    if sqdump is None:
        sqdump = bufs[0]
    stats = []
    for i, (src_ap, _) in enumerate(items):
        ssq = st[:, stc0 + 6 * i: stc0 + 6 * i + 1]
        sx = st[:, stc0 + 6 * i + 1: stc0 + 6 * i + 2]
        C.op("scalar", lambda e, _s=sqdump, _x=src_ap, _p=ssq:
             e.activation(_s, _x, AF.Square, accum_out=_p), ss, after=(ss,))
        C.op("vector", lambda e, _o=sx, _x=src_ap: e.reduce_sum(_o, _x, AX.X),
             sv, after=(sv,))
        stats.append((ssq, sx, ss.n, sv.n))
    buf_free = {}
    for i, (src_ap, dst) in enumerate(items):
        ssq, sx, ssm, svm = stats[i]
        cb = stc0 + 6 * i
        m = st[:, cb + 2: cb + 3]
        var = st[:, cb + 3: cb + 4]
        r = st[:, cb + 4: cb + 5]
        b = st[:, cb + 5: cb + 6]
        C.wait("vector", ss, ssm)
        C.op("vector", lambda e, _o=m, _x=sx: e.tensor_scalar_mul(_o, _x, 1.0 / D),
             sv, after=(sv,))
        C.op("vector", lambda e, _o=var, _x=ssq: e.tensor_scalar_mul(_o, _x, 1.0 / D),
             sv, after=(sv,))
        C.op("vector", lambda e, _o=b, _m=m: e.tensor_tensor(_o, _m, _m, ALU.mult),
             sv, after=(sv,))
        C.op("vector", lambda e, _o=var, _b=b: e.tensor_sub(_o, _o, _b),
             sv, after=(sv,))
        C.op("vector", lambda e, _o=var: e.tensor_scalar_add(_o, _o, EPS),
             sv, after=(sv,))
        C.wait("scalar", sv)
        C.op("scalar", lambda e, _v=var: e.sqrt(_v, _v), ss)
        C.wait("vector", ss)
        C.op("vector", lambda e, _o=r, _v=var: e.reciprocal(_o, _v),
             sv, after=(sv,))
        C.op("vector", lambda e, _o=b, _m=m, _r=r: e.tensor_tensor(_o, _m, _r, ALU.mult),
             sv, after=(sv,))
        C.op("vector", lambda e, _o=b: e.tensor_scalar_mul(_o, _o, -1.0),
             sv, after=(sv,))
        buf = bufs[i % len(bufs)]
        if i % len(bufs) in buf_free:
            _sm, _vv = buf_free[i % len(bufs)]
            C.wait("scalar", _sm, _vv)
        C.wait("scalar", sv)
        C.op("scalar", lambda e, _o=buf, _x=src_ap, _b=b, _r=r:
             e.activation(_o, _x, AF.Identity, bias=_b, scale=_r), ss)
        if tr is None:
            C.wait("vector", ss)
            C.op("vector", lambda e, _o=dst, _s=buf: e.tensor_add(_o, _o, _s),
                 sv, after=(sv,))
            buf_free[i % len(bufs)] = (sv, sv.n)
        else:
            buf_free[i % len(bufs)] = tr(i, buf)


def emit_ln1_fm(C, qbs, tag, sv, ss, stt):
    sb = C.sb
    xr, xnf, abf = sb["xr"], sb["xnf"], sb["abf"]
    XNTM = 32768
    tokw = 128 * len(qbs)
    evacs = []

    def tr(qi, buf):
        C.wait("tensor", ss)
        for ch in range(16):
            t = qi * 16 + ch
            bank = C.ps[6 + (t % 2)]
            pv = bank.bitcast(BF16)[:, 0:128]
            if len(evacs) >= 2:
                C.wait("tensor", sv, evacs[-2])
            C.op("tensor", lambda e, _p=pv, _i=buf[:, ch * 128:(ch + 1) * 128]:
                 e.transpose(_p, _i, sb["ident"][:, :]), stt)
            C.wait("vector", stt)
            dst = xnf[:, ch * tokw + qi * 128: ch * tokw + qi * 128 + 128]
            C.op("vector", lambda e, _o=dst, _p=pv: e.tensor_copy(_o, _p), sv)
            evacs.append(C.sems[sv.name].n)
        return (stt, stt.n)

    items = [(xr[:, qb * D:(qb + 1) * D], None) for qb in qbs]
    emit_ln_pipe(C, items, sv, ss,
                 bufs=[abf[:, XNTM:XNTM + D], abf[:, XNTM + D:XNTM + 2 * D]],
                 sqdump=af32v(C, 1024, D), tr=tr)


def emit_ffn(C, tag, gu_src, down_src, fwht, cast_gate=None):
    sb = C.sb
    xr, xnf, abf = sb["xr"], sb["xnf"], sb["abf"]
    NHC = 64
    HID_LO = 0
    WPA = 24576
    WPB = 28672
    SP0 = 34816
    sv = C.sem(tag + "v")
    ss = C.sem(tag + "s")
    stt = C.sem(tag + "t")

    def hidc(c):
        if c < 16:
            return xnf[:, 8192 + c * 512: 8192 + (c + 1) * 512]
        return abf[:, HID_LO + (c - 16) * 512: HID_LO + (c - 15) * 512]

    spare = [abf[:, SP0:SP0 + 512], abf[:, SP0 + 512:SP0 + 1024]]

    def guslot(slot):
        return abf[:, WPA + slot * 4096: WPA + (slot + 1) * 4096]

    ps_up = PanelStream(C, f"{tag}up")
    ps_dn = PanelStream(C, f"{tag}dn", nbuf=8)

    def dsl(c):
        off = WPA + (c % 8) * 1024
        return abf[:, off:off + 1024]

    for tile in range(DEBUG_TILES):
        qbs = [tile * 4 + i for i in range(4)]
        emit_ln1_fm(C, qbs, tag, sv, ss, stt)
        if DEBUG_STAGE < 2:
            continue
        bf_ready = {c: [] for c in range(NHC)}
        if fwht:
            slotmap = list(range(NHC))
            slots = {i: hidc(i) for i in range(NHC)}
            slots[64], slots[65] = spare[0], spare[1]
            free = [64, 65]
            plan = []
            for bit in range(6):
                step = 1 << bit
                for c0 in range(NHC):
                    if c0 & step:
                        continue
                    rdy = min(c0 | (2 * step - 1), NHC - 1)
                    plan.append((rdy, bit, c0))
            plan.sort(key=lambda t: (t[0], t[1], t[2]))
            for rdy, bit, c0 in plan:
                c1 = c0 | (1 << bit)
                a_s, b_s = slotmap[c0], slotmap[c1]
                d0, d1 = free
                bf_ready[rdy].append((slots[d0], slots[a_s], slots[b_s], True))
                bf_ready[rdy].append((slots[d1], slots[a_s], slots[b_s], False))
                slotmap[c0], slotmap[c1] = d0, d1
                free = [a_s, b_s]
            getc = lambda c, _sl=slots, _sm=list(slotmap): _sl[_sm[c]]
        else:
            getc = hidc
        def up_load(c):
            slot = ps_up.k % 2
            ps_up.load([(guslot(slot), gu_src[c * 128:(c + 1) * 128, :])])
        up_load(0)
        up_load(1)
        C.wait("tensor", sv)
        prod_marks = []
        h128_marks = []
        for c in range(NHC):
            slot = c % 2
            bg, bu = C.ps[0 + 2 * slot], C.ps[1 + 2 * slot]
            if c + 2 < NHC:
                up_load(c + 2)
            ps_up.use()
            if len(prod_marks) >= 2:
                C.wait("tensor", sv, prod_marks[-2])
            for dc in range(16):
                C.op("tensor", lambda e, _b=bg,
                     _w=guslot(slot)[:, dc * 128:(dc + 1) * 128],
                     _x=xnf[:, dc * 512:(dc + 1) * 512], _st=(dc == 0), _sp=(dc == 15):
                     e.matmul(_b.ap(), _w, _x, start=_st, stop=_sp))
            for dc in range(16):
                C.op("tensor", lambda e, _b=bu,
                     _w=guslot(slot)[:, 2048 + dc * 128: 2048 + (dc + 1) * 128],
                     _x=xnf[:, dc * 512:(dc + 1) * 512], _st=(dc == 0), _sp=(dc == 15):
                     e.matmul(_b.ap(), _w, _x, start=_st, stop=_sp),
                     ps_up.fr if dc == 15 else None)
            ps_up.done_attached()
            sil = af32v(C, 512 * (c % 2), 512)
            C.wait("scalar", ps_up.fr)
            if c >= 2:
                C.wait("scalar", sv, prod_marks[-2] if len(prod_marks) >= 2 else 0)
            C.op("scalar", lambda e, _s=sil, _b=bg: e.activation(_s, _b.ap(), AF.Silu), ss,
                 after=(ss,))
            C.wait("vector", ss)
            C.op("vector", lambda e, _o=hidc(c), _s=sil, _b=bu:
                 e.tensor_tensor(_o, _s, _b.ap(), ALU.mult), sv)
            prod_marks.append(C.sems[sv.name].n)
            if fwht:
                hb = C.ps[4 + (c % 2)]
                if c >= 2:
                    C.wait("tensor", sv, h128_marks[c - 2])
                C.wait("tensor", sv, prod_marks[-1])
                C.op("tensor", lambda e, _b=hb, _h=hidc(c):
                     e.matmul(_b.ap(), sb["h128"][:, :], _h, start=True,
                              stop=True), stt)
                C.wait("vector", stt)
                C.op("vector", lambda e, _o=hidc(c), _b=hb:
                     e.tensor_copy(_o, _b.ap()), sv, after=(sv,))
                h128_marks.append(C.sems[sv.name].n)
                for _o, _a, _bb, _add in bf_ready[c]:
                    if _add:
                        C.op("vector", lambda e, _oo=_o, _aa=_a, _b2=_bb:
                             e.tensor_add(_oo, _aa, _b2), sv, after=(sv,))
                    else:
                        C.op("vector", lambda e, _oo=_o, _aa=_a, _b2=_bb:
                             e.tensor_sub(_oo, _aa, _b2), sv)
        if DEBUG_STAGE < 3:
            continue
        if DEBUG_STAGE < 4:
            continue
        zt = xnf
        st = C.sb["stats"]
        C.wait("gpsimd", ps_up.fr)
        for half in range(2):
            for c in range(NHC):
                ps_dn.load([(dsl(c), down_src[(half * 64 + c) * 128:
                                              (half * 64 + c + 1) * 128, :])])
            C.wait("tensor", sv)
            for c in range(NHC):
                ps_dn.use()
                for qc in range(4):
                    for sl in range(2):
                        bank = C.ps[qc * 2 + sl]
                        C.op("tensor", lambda e, _b=bank,
                             _l=getc(c)[:, qc * 128:(qc + 1) * 128],
                             _r=dsl(c)[:, sl * 512:(sl + 1) * 512],
                             _st=(c == 0), _sp=(c == NHC - 1):
                             e.matmul(_b.ap(), _l, _r, start=_st, stop=_sp),
                             ps_dn.fr if (qc == 3 and sl == 1) else None)
                ps_dn.done_attached()
            C.wait("scalar", ps_dn.fr)
            for qc in range(4):
                for sl in range(2):
                    bank = C.ps[qc * 2 + sl]
                    dout0 = half * 1024 + sl * 512
                    zsl = zt[:, qc * 2048 + dout0: qc * 2048 + dout0 + 512]
                    pz = st[:, 16 + qc * 4 + half * 2 + sl: 17 + qc * 4 + half * 2 + sl]
                    C.op("scalar", lambda e, _z=zsl, _b=bank, _p=pz:
                         e.activation(_z, _b.ap(), AF.Identity, accum_out=_p), ss,
                         after=(ss,))
            if half == 0:
                C.wait("tensor", ss)
        if DEBUG_STAGE < 5:
            continue
        C.wait("vector", ss)
        for qi, qb in enumerate(qbs):
            zqb = zt[:, qi * 2048:(qi + 1) * 2048]
            sx = st[:, 1:2]
            ssq = st[:, 0:1]
            C.op("vector", lambda e, _o=sx, _a=st[:, 16 + qi * 4:17 + qi * 4],
                 _b=st[:, 17 + qi * 4:18 + qi * 4]: e.tensor_add(_o, _a, _b), sv, after=(sv,))
            C.op("vector", lambda e, _o=sx, _b=st[:, 18 + qi * 4:19 + qi * 4]:
                 e.tensor_add(_o, _o, _b), sv, after=(sv,))
            C.op("vector", lambda e, _o=sx, _b=st[:, 19 + qi * 4:20 + qi * 4]:
                 e.tensor_add(_o, _o, _b), sv, after=(sv,))
            sqd = af32v(C, 1024, D)
            C.wait("scalar", sv)
            C.op("scalar", lambda e, _s=sqd, _z=zqb, _p=ssq:
                 e.activation(_s, _z, AF.Square, accum_out=_p), ss, after=(ss,))
            C.wait("vector", ss)
            emit_ln(C, zqb, sv, ss, res_add=xr[:, qb * D:(qb + 1) * D],
                    presq=ssq, presum=sx)
        C.wait("scalar", sv)
        C.wait("tensor", sv)
        C.wait("gpsimd", sv)
        C.wait("sync", sv)


def pieces_of(jp):
    if jp <= 3:
        return [(jp * 128, 512 - jp * 128), (512, 512)]
    return [(jp * 128, 1024 - jp * 128)]


POFS = [0, 1024, 1920, 2688, 3328, 3840, 4224, 4480]


def emit_attn_head(C, P, h, h8, prev_tail=None):
    abf = C.sb["abf"]
    mtri, onesc, onesr, bfs = C.sb["mtri"], C.sb["onesc"], C.sb["onesr"], C.sb["bfs"]
    sv, ss, stt, spe, cc = P["sv"], P["ss"], P["stt"], P["spe"], P["cc"]
    ps_k, ps_v = P["ps_k"], P["ps_v"]
    k_gath, v_gath, kgr = P["k_gath"], P["v_gath"], P["kgr"]
    negM = P["negM"]
    first_sync_kv = P["fskv"]
    pm = P.setdefault("pm", {})
    srow = C.sb["af32"][0:1, 0:1024]
    KP, VP, PP, QG, AO = 8192, 10240, 12288, 0, 20992
    if first_sync_kv[0]:
        C.wait("sync", cc, 2)
    ps_k.load([(abf[:, KP + (ps_k.k % 2) * 1024: KP + (ps_k.k % 2 + 1) * 1024],
                k_gath.ap()[0 * kgr + h * 128: 0 * kgr + (h + 1) * 128, :])])
    if first_sync_kv[0]:
        C.wait("sync", cc, 3)
        first_sync_kv[0] = False
    ps_v.load([(abf[:, VP + (ps_v.k % 2) * 1024: VP + (ps_v.k % 2 + 1) * 1024]
                .rearrange("p (jb d) -> p jb d", jb=8),
                v_gath.ap()[0 * 1024:1 * 1024, h * 128:(h + 1) * 128]
                .rearrange("(jb p) d -> p jb d", p=128))])
    ktok = [ps_k.pending[-1]]
    vtok = [ps_v.pending[-1]]
    for r in range(1, 8):
        ps_k.load([(abf[:, KP + (ps_k.k % 2) * 1024: KP + (ps_k.k % 2 + 1) * 1024],
                    k_gath.ap()[r * kgr + h * 128: r * kgr + (h + 1) * 128, :])])
        ps_v.load([(abf[:, VP + (ps_v.k % 2) * 1024: VP + (ps_v.k % 2 + 1) * 1024]
                    .rearrange("p (jb d) -> p jb d", jb=8),
                    v_gath.ap()[r * 1024:(r + 1) * 1024, h * 128:(h + 1) * 128]
                    .rearrange("(jb p) d -> p jb d", p=128))])
        ktok.append(ps_k.pending[-1])
        vtok.append(ps_v.pending[-1])
    exp_marks = {}
    mask_marks = {}
    sum_first = {6: True, 7: True}
    pv_first = {0: True, 1: True}

    def block(b):
        r, jp = divmod(b, 8)
        if jp == 0:
            C.wait("tensor", ps_k.ld[ktok[r][0]], ktok[r][1])
            C.wait("tensor", ps_v.ld[vtok[r][0]], vtok[r][1])
        pid = 1 + (b % 2)
        pr = C.pr[pid]
        if pid in pm:
            C.wait("tensor", ss, pm[pid])
        kslot = abf[:, KP + (r % 2) * 1024: KP + (r % 2 + 1) * 1024]
        sc_mark = None
        for pi, (off, w) in enumerate(pieces_of(jp)):
            C.op("tensor", lambda e, _pr=pr, _o=off, _w=w,
                 _k=kslot[:, jp * 128:(jp + 1) * 128],
                 _q=abf[:, QG + h8 * 1024 + off: QG + h8 * 1024 + off + w]:
                 e.matmul(_pr.ap()[:, _o:_o + _w], _k, _q, start=True, stop=True),
                 stt)
            sc_mark = stt.n
        wq = 1024 - jp * 128
        C.wait("scalar", stt, sc_mark)
        C.op("scalar", lambda e, _p=abf[:, PP + POFS[jp]: PP + POFS[jp] + wq],
             _i=pr.ap()[:, jp * 128:1024], _n=negM():
             e.activation(_p, _i, AF.Exp, bias=_n), ss)
        if jp == 7:
            ps_k.done("scalar")
        exp_marks[b] = ss.n
        pm[pid] = ss.n
        C.wait("vector", ss, exp_marks[b])
        C.op("vector", lambda e, _p=abf[:, PP + POFS[jp]: PP + POFS[jp] + 128],
             _m=mtri[:, r * 128:(r + 1) * 128]:
             e.tensor_mul(_p, _p, _m), sv, after=(sv,))
        mask_marks[b] = sv.n

    def consume(b):
        r, jp = divmod(b, 8)
        vslot = abf[:, VP + (r % 2) * 1024: VP + (r % 2 + 1) * 1024]
        C.wait("tensor", ss, exp_marks[b])
        C.wait("tensor", sv, mask_marks[b])
        for pi, (off, w) in enumerate(pieces_of(jp)):
            bank = 6 if off < 512 else 7
            bstop = (r == 7) and ((jp == 3 and off < 512) or
                                  (jp == 7 and off >= 512))
            pp = abf[:, PP + POFS[jp] + (off - jp * 128):
                     PP + POFS[jp] + (off - jp * 128) + w]
            C.op("tensor", lambda e, _o=off, _w=w, _p=pp,
                 _st=sum_first[bank], _sp=bstop:
                 e.matmul(C.pr[3].ap()[0:1, _o:_o + _w], onesc[:, :], _p,
                          start=_st, stop=_sp),
                 spe if bstop else None)
            sum_first[bank] = False
            obank = 0 if off < 512 else 1
            lastpv = (jp == 7 and pi == len(pieces_of(jp)) - 1)
            C.op("tensor", lambda e, _o=off, _w=w, _p=pp,
                 _v=vslot[:, jp * 128:(jp + 1) * 128],
                 _st=pv_first[obank], _sp=bstop:
                 e.matmul(C.pr[0].ap()[:, _o:_o + _w], _v, _p, start=_st, stop=_sp),
                 ps_v.fr if lastpv else None)
            pv_first[obank] = False
        if jp == 7:
            ps_v.done_attached()

    if prev_tail is None:
        C.wait("tensor", sv)
    block(0)
    block(1)
    if prev_tail is not None:
        prev_tail()
    C.wait("tensor", sv)
    for b in range(2, 64):
        block(b)
        consume(b - 2)
    consume(62)
    consume(63)

    def tail():
        C.wait("vector", spe)
        C.wait("vector", ps_v.fr)
        C.op("vector", lambda e: e.tensor_copy(srow, C.pr[3].ap()[0:1, 0:1024]),
             sv, after=(sv,))
        C.op("vector", lambda e: e.tensor_scalar_add(srow, srow, 1e-30),
             sv, after=(sv,))
        C.op("vector", lambda e: e.tensor_copy(bfs[0:1, 1024:2048], srow),
             sv, after=(sv,))
        C.wait("tensor", sv)
        for i in range(2):
            C.op("tensor", lambda e, _i=i: e.matmul(
                C.ps[6 + _i].ap(), onesr[:, :],
                bfs[0:1, 1024 + _i * 512: 1024 + (_i + 1) * 512],
                start=True, stop=True), stt)
        C.wait("vector", stt)
        rb = af32v(C, 4096, 1024)
        C.op("vector", lambda e: e.tensor_copy(rb, C.pr[3].ap()), sv, after=(sv,))
        C.op("vector", lambda e: e.reciprocal(rb, rb), sv, after=(sv,))
        C.op("vector", lambda e, _h=h: e.tensor_mul(
            abf[:, AO + _h * 1024: AO + (_h + 1) * 1024],
            C.pr[0].ap(), rb), sv, after=(sv,))

    return tail


def emit_outproj_ln2(C, tag, wsrc, NH, bo1, prev_fr, pan=False, cast_gate=None):
    sb = C.sb
    xr, abf, bfs = sb["xr"], sb["abf"], sb["bfs"]
    onesr = sb["onesr"]
    af = sb["af32"]
    borow = af[0:1, 2048:4096]
    WQP, AO = 16896, 20992
    sv = C.sem(tag + "v")
    ss = C.sem(tag + "s")
    stt = C.sem(tag + "t")
    spe = C.sem(tag + "pe")
    sbi = C.sem(tag + "bi")
    C.wait("sync", sv)
    C.wait("sync", ss)
    C.dma("sync", borow, bo1, sbi)
    C.wait("vector", sbi)
    C.op("vector", lambda e: e.tensor_copy(bfs[0:1, 0:2048], borow), sv, after=(sv,))
    C.wait("tensor", sv)
    _bcm2 = []
    for i in range(4):
        if i >= 2:
            C.wait("tensor", sv, _bcm2[i - 2])
        C.op("tensor", lambda e, _i=i: e.matmul(
            C.ps[4 + (_i % 2)].ap(), onesr[:, :], bfs[0:1, _i * 512: (_i + 1) * 512],
            start=True, stop=True), stt)
        C.wait("vector", stt)
        C.op("vector", lambda e, _i=i: e.tensor_copy(
            af32v(C, 2048 + _i * 512, 512), C.ps[4 + (_i % 2)].ap()), sv)
        _bcm2.append(C.sems[tag + "v"].n)
    bo_b = af32v(C, 2048, 2048)

    ps_wo = PanelStream(C, tag + "wo", nbuf=4)
    zt = abf[:, 0:16384]
    C.wait("gpsimd", prev_fr)

    def wo_load(sl, h):
        dst = abf[:, WQP + (ps_wo.k % 8) * 512: WQP + (ps_wo.k % 8) * 512 + 512]
        if pan:
            r0 = (sl * NH + h) * 128
            ps_wo.load([(dst, wsrc[r0:r0 + 128, :])])
        else:
            ps_wo.load([(dst, wsrc[h * 128:(h + 1) * 128, sl * 512:(sl + 1) * 512])])

    wo_load(0, 0)
    wo_load(0, 1)
    C.wait("tensor", sv)
    zev_marks = {}
    for sl in range(4):
        for h in range(NH):
            nxt = sl * NH + h + 2
            if nxt < 4 * NH:
                wo_load(nxt // NH, nxt % NH)
            ps_wo.use()
            if sl > 0 and h == 0:
                C.wait("tensor", sv, zev_marks[sl - 1])
            for qb in range(8):
                wsl = abf[:, WQP + (ps_wo.uk % 8) * 512: WQP + (ps_wo.uk % 8) * 512 + 512]
                carry = None
                if qb == 7:
                    carry = ps_wo.fr
                elif h == NH - 1:
                    carry = spe
                C.op("tensor", lambda e, _b=C.ps[qb],
                     _l=abf[:, AO + h * 1024 + qb * 128: AO + h * 1024 + qb * 128 + 128],
                     _r=wsl, _st=(h == 0), _sp=(h == NH - 1):
                     e.matmul(_b.ap(), _l, _r, start=_st, stop=_sp), carry)
            ps_wo.done_attached()
        C.wait("vector", spe)
        C.wait("vector", ps_wo.fr)
        for qb in range(8):
            C.op("vector", lambda e, _z=zt[:, qb * 2048 + sl * 512: qb * 2048 + sl * 512 + 512],
                 _b=C.ps[qb], _bo=bo_b[:, sl * 512:(sl + 1) * 512]:
                 e.tensor_add(_z, _b.ap(), _bo), sv, after=(sv,))
        zev_marks[sl] = sv.n

    C.wait("scalar", sv)
    emit_ln_pipe(C, [(zt[:, qb * 2048:(qb + 1) * 2048],
                      xr[:, qb * D:(qb + 1) * D]) for qb in range(8)],
                 sv, ss, bufs=[af32v(C, 0, D), af32v(C, 2048, D)])


def emit_attn0(C, tag, wqT, wkT, wvT, woT, bq2, bk2, bv1, bo1, mtri_in,
               cg,
               k_loc, v_loc, k_gath, v_gath, knm_loc, knm_gath):
    sb = C.sb
    xr, xnf, abf = sb["xr"], sb["xnf"], sb["abf"]
    st, pbias, bfs, bcol = sb["stats"], sb["pbias"], sb["bfs"], sb["bcol"]
    af = sb["af32"]
    bvrow = af[0:1, 2048:4096]
    nrow = af[0:1, 2048:2560]
    knmrow = af[0:1, 2560:2576]
    kgrow = af[0:1, 2688:2816]
    srow = af[0:1, 0:1024]
    borow = af[0:1, 2048:4096]
    mtri, onesc, onesr = sb["mtri"], sb["onesc"], sb["onesr"]
    QG, KP, VP, PP, WQP, AO = 0, 8192, 10240, 12288, 16896, 20992
    sv = C.sem(tag + "v")
    ss = C.sem(tag + "s")
    stt = C.sem(tag + "t")
    spe = C.sem(tag + "pe")
    cc = C.sem(tag + "cc")
    ksm = [C.sem(tag + f"ks{i}") for i in range(4)]
    ksn = C.sem(tag + "ksn")
    vsm = [C.sem(tag + f"vs{i}") for i in range(2)]
    sdm = C.sem(tag + "dm")
    sbi = C.sem(tag + "bi")

    def wk_slot(i):
        return abf[:, WQP + (i % 2) * 2048: WQP + (i % 2 + 1) * 2048]

    def wv_slot(i):
        return abf[:, WQP + (i % 4) * 512: WQP + (i % 4 + 1) * 512]

    emit_ln1_fm(C, list(range(8)), tag, sv, ss, stt)
    C.dma("sync", mtri[:, :], mtri_in, sdm)

    fq = af32v(C, 0, 128)
    fk = af32v(C, 128, 128)
    C.wait("sync", ss)
    C.wait("sync", sv)
    C.dma("sync", fq[0:16, :], bq2, sbi)
    C.dma("sync", fk[0:16, :], bk2, sbi)
    C.dma("sync", bvrow, bv1, sbi)
    C.wait("vector", sbi)
    C.wait("vector", ss)
    C.op("vector", lambda e: e.tensor_scalar_mul(fq[0:16, :], fq[0:16, :], SCALE0),
         sv, after=(sv,))
    C.op("vector", lambda e: e.tensor_copy(bfs[0:16, 0:128], fq[0:16, :]), sv, after=(sv,))
    C.op("vector", lambda e: e.tensor_scalar_mul(fk[0:16, :], fk[0:16, :], SCALE0),
         sv, after=(sv,))
    C.op("vector", lambda e: e.tensor_copy(bfs[0:16, 128:256], fk[0:16, :]), sv, after=(sv,))
    C.wait("tensor", sv)
    bqp = C.ps[6].bitcast(BF16)[:, 0:16]
    bkp = C.ps[7].bitcast(BF16)[:, 0:16]
    C.op("tensor", lambda e: e.transpose(bqp, bfs[0:16, 0:128], sb["ident"][0:16, 0:16]), stt)
    C.op("tensor", lambda e: e.transpose(bkp, bfs[0:16, 128:256], sb["ident"][0:16, 0:16]), stt)
    C.wait("vector", stt)
    C.op("vector", lambda e: e.tensor_copy(bcol[:, 0:16], bqp), sv, after=(sv,))
    C.op("vector", lambda e: e.tensor_copy(bcol[:, 16:32], bkp), sv, after=(sv,))
    C.wait("vector", stt)
    C.op("vector", lambda e: e.tensor_copy(bfs[0:1, 0:2048], bvrow), sv, after=(sv,))
    C.wait("tensor", sv)
    _bcm = []
    for i in range(4):
        if i >= 2:
            C.wait("tensor", sv, _bcm[i - 2])
        C.op("tensor", lambda e, _i=i: e.matmul(
            C.ps[4 + (_i % 2)].ap(), onesr[:, :],
            bfs[0:1, _i * 512: (_i + 1) * 512],
            start=True, stop=True), stt)
        C.wait("vector", stt)
        C.op("vector", lambda e, _i=i: e.tensor_copy(
            af32v(C, _i * 512, 512), C.ps[4 + (_i % 2)].ap()), sv)
        _bcm.append(C.sems[tag + "v"].n)
    bv_b = af32v(C, 0, 2048)

    ps_wk = PanelStream(C, tag + "wk")
    ksq = abf[:, PP:PP + 1024]

    def kw_load(h, wsrc):
        ps_wk.load([(wk_slot(ps_wk.k), wsrc[h * 128:(h + 1) * 128, :])])

    kw_load(0, wkT)
    kw_load(1, wkT)
    C.wait("tensor", sv)
    ev_marks = {}
    mm_marks = {}
    sq_marks = {}
    kst_marks = {}
    ncp_marks = {}
    for h in range(16):
        slot = h % 2
        if h + 2 < 16:
            kw_load(h + 2, wkT)
        ps_wk.use()
        if h >= 2:
            C.wait("tensor", ss, ev_marks[h - 2])
        for nh in range(2):
            for dc in range(16):
                last = (nh == 1 and dc == 15)
                C.op("tensor", lambda e, _b=C.ps[2 * slot + nh],
                     _w=wk_slot(slot)[:, dc * 128:(dc + 1) * 128],
                     _x=xnf[:, dc * 1024 + nh * 512: dc * 1024 + nh * 512 + 512],
                     _st=(dc == 0), _sp=(dc == 15):
                     e.matmul(_b.ap(), _w, _x, start=_st, stop=_sp),
                     (ps_wk.fr if last else (stt if dc == 15 else None)))
        ps_wk.done_attached()
        kp = abf[:, KP + (h % 4) * 1024: KP + (h % 4 + 1) * 1024]
        C.wait("scalar", stt)
        C.wait("scalar", ps_wk.fr)
        if h >= 4:
            _sm, _vv = kst_marks[h - 4]
            C.wait("scalar", _sm, _vv)
        for nh in range(2):
            C.op("scalar", lambda e, _o=kp[:, nh * 512:(nh + 1) * 512],
                 _b=C.ps[2 * slot + nh], _bias=bcol[:, 16 + h:17 + h]:
                 e.activation(_o, _b.ap(), AF.Identity, bias=_bias, scale=SCALE0),
                 ss, after=(ss,))
        ev_marks[h] = ss.n
        C.wait("sync", ss, ev_marks[h])
        C.dma("sync", k_loc.ap()[h * 128:(h + 1) * 128, :], kp, ksm[h % 4])
        kst_marks[h] = (ksm[h % 4], ksm[h % 4].n)
        if h >= 1:
            C.wait("scalar", stt, mm_marks[h - 1])
        C.op("scalar", lambda e, _kp=kp: e.activation(ksq, _kp, AF.Square), ss, after=(ss,))
        sq_marks[h] = ss.n
        C.wait("tensor", ss, sq_marks[h])
        if h >= 2:
            C.wait("tensor", sv, ncp_marks[h - 2])
        for nh in range(2):
            C.op("tensor", lambda e, _b=C.ps[4 + slot],
                 _r=ksq[:, nh * 512:(nh + 1) * 512], _st=(nh == 0), _sp=(nh == 1):
                 e.matmul(_b.ap()[0:1, 0:512], onesc[:, :], _r,
                          start=_st, stop=_sp), stt)
        mm_marks[h] = stt.n
        C.wait("vector", stt, mm_marks[h])
        C.op("vector", lambda e, _s=slot: e.tensor_copy(nrow, C.ps[4 + _s].ap()[0:1, 0:512]),
             sv, after=(sv,))
        ncp_marks[h] = sv.n
        C.op("vector", lambda e: e.reduce_max(st[0:1, 33:34], nrow, AX.X),
             sv, after=(sv,))
        if h == 0:
            C.op("vector", lambda e: e.tensor_copy(st[0:1, 32:33], st[0:1, 33:34]),
                 sv, after=(sv,))
        else:
            C.op("vector", lambda e: e.tensor_max(st[0:1, 32:33], st[0:1, 32:33], st[0:1, 33:34]),
                 sv, after=(sv,))
    C.op("vector", lambda e: e.memset(knmrow, 0.0), sv, after=(sv,))
    C.op("vector", lambda e: e.tensor_copy(knmrow[0:1, 0:1], st[0:1, 32:33]), sv, after=(sv,))
    C.wait("sync", sv)
    C.dma("sync", knm_loc.ap(), knmrow, ksn)
    ps_wv = PanelStream(C, tag + "wv", nbuf=4)
    C.wait("gpsimd", ps_wk.fr)

    def vw_load(hg, dc):
        r0 = (hg * 16 + dc) * 128
        ps_wv.load([(wv_slot(ps_wv.k), wvT[r0:r0 + 128, :])])

    vst_marks = {}
    vev_marks = {}
    for pre in range(4):
        vw_load(0, pre)
    for _k in ksm:
        C.wait("gpsimd", _k)
    C.wait("gpsimd", ksn)
    C.op("gpsimd", lambda e: e.collective_compute(
        "AllGather", ALU.bypass, replica_groups=[list(range(NCORES))],
        ins=[knm_loc.ap()], outs=[knm_gath.ap()]), cc)
    C.op("gpsimd", lambda e: e.collective_compute(
        "AllGather", ALU.bypass, replica_groups=[list(range(NCORES))],
        ins=[k_loc.ap()], outs=[k_gath.ap()]), cc)
    C.wait("tensor", sv)
    for hg in range(4):
        for dc in range(16):
            if not (hg == 0 and dc < 4):
                pass
            nxt = hg * 16 + dc + 4
            if nxt < 64:
                vw_load((nxt) // 16, (nxt) % 16)
            ps_wv.use()
            if hg > 0 and dc == 0:
                C.wait("tensor", sv, vev_marks[hg - 1])
            for qb in range(8):
                last = (qb == 7)
                C.op("tensor", lambda e, _b=C.ps[qb],
                     _l=xnf[:, dc * 1024 + qb * 128: dc * 1024 + qb * 128 + 128],
                     _r=wv_slot(ps_wv.uk), _st=(dc == 0), _sp=(dc == 15):
                     e.matmul(_b.ap(), _l, _r, start=_st, stop=_sp),
                     (ps_wv.fr if last else (stt if (dc == 15 and qb == 7) else None)))
            ps_wv.done_attached()
        C.wait("vector", ps_wv.fr)
        strip = abf[:, QG + (hg % 2) * 4096: QG + (hg % 2) * 4096 + 4096]
        if hg >= 2:
            _sm, _vv = vst_marks[hg - 2]
            C.wait("vector", _sm, _vv)
        for qb in range(8):
            C.op("vector", lambda e, _o=strip[:, qb * 512:(qb + 1) * 512],
                 _b=C.ps[qb], _bias=bv_b[:, hg * 512:(hg + 1) * 512]:
                 e.tensor_add(_o, _b.ap(), _bias), sv, after=(sv,))
        C.wait("sync", sv)
        C.dma("sync",
              v_loc.ap()[:, hg * 512:(hg + 1) * 512]
              .rearrange("(qb p) d -> p qb d", p=128),
              strip.rearrange("p (qb d) -> p qb d", qb=8), vsm[hg % 2])
        vst_marks[hg] = (vsm[hg % 2], vsm[hg % 2].n)
        vev_marks[hg] = sv.n
    ps_wq = PanelStream(C, tag + "wq")
    ps_k = PanelStream(C, tag + "gk", dma_eng="sync")
    ps_v = PanelStream(C, tag + "gv", dma_eng="sync")
    qsq = abf[:, PP:PP + 1024]
    C.wait("gpsimd", ps_wv.fr)

    def qw_load(h):
        ps_wq.load([(wk_slot(ps_wq.k), wqT[h * 128:(h + 1) * 128, :])])

    qw_load(0)
    qw_load(1)
    for _v in vsm:
        C.wait("gpsimd", _v)
    C.op("gpsimd", lambda e: e.collective_compute(
        "AllGather", ALU.bypass, replica_groups=[list(range(NCORES))],
        ins=[v_loc.ap()], outs=[v_gath.ap()]), cc)
    C.wait("scalar", vsm[0])
    C.wait("scalar", vsm[1])
    first_sync_kv = [True]

    ao_marks = {}
    for g in range(2):
        if g == 1:
            qw_load(g * 8)
            qw_load(g * 8 + 1)
        C.wait("tensor", sv)
        C.wait("scalar", stt)
        C.wait("scalar", spe)
        C.wait("scalar", C.sems[tag + "gv_f"])
        qev = {}
        qmm = {}
        qncp = {}
        for h8 in range(8):
            h = g * 8 + h8
            slot = h8 % 2
            if h8 + 2 < 8:
                qw_load(h + 2)
            ps_wq.use()
            if h8 >= 2:
                C.wait("tensor", ss, qev[h8 - 2])
            for nh in range(2):
                for dc in range(16):
                    last = (nh == 1 and dc == 15)
                    C.op("tensor", lambda e, _b=C.ps[2 * slot + nh],
                         _w=wk_slot(ps_wq.uk)[:, dc * 128:(dc + 1) * 128],
                         _x=xnf[:, dc * 1024 + nh * 512: dc * 1024 + nh * 512 + 512],
                         _st=(dc == 0), _sp=(dc == 15):
                         e.matmul(_b.ap(), _w, _x, start=_st, stop=_sp),
                         (ps_wq.fr if last else (stt if dc == 15 else None)))
            ps_wq.done_attached()
            C.wait("scalar", stt)
            C.wait("scalar", ps_wq.fr)
            for nh in range(2):
                C.op("scalar", lambda e,
                     _o=abf[:, QG + h8 * 1024 + nh * 512: QG + h8 * 1024 + nh * 512 + 512],
                     _b=C.ps[2 * slot + nh], _bias=bcol[:, h:h + 1]:
                     e.activation(_o, _b.ap(), AF.Identity, bias=_bias, scale=SCALE0),
                     ss, after=(ss,))
            qev[h8] = ss.n
            if h8 >= 1:
                C.wait("scalar", stt, qmm[h8 - 1])
            C.op("scalar", lambda e, _q=abf[:, QG + h8 * 1024: QG + (h8 + 1) * 1024]:
                 e.activation(qsq, _q, AF.Square), ss, after=(ss,))
            C.wait("tensor", ss)
            if h8 >= 2:
                C.wait("tensor", sv, qncp[h8 - 2])
            for nh in range(2):
                C.op("tensor", lambda e, _b=C.ps[4 + slot],
                     _r=qsq[:, nh * 512:(nh + 1) * 512], _st=(nh == 0), _sp=(nh == 1):
                     e.matmul(_b.ap()[0:1, 0:512], onesc[:, :], _r,
                              start=_st, stop=_sp), stt)
            qmm[h8] = stt.n
            C.wait("vector", stt, qmm[h8])
            C.op("vector", lambda e, _b=C.ps[4 + slot]:
                 e.tensor_copy(nrow, _b.ap()[0:1, 0:512]), sv, after=(sv,))
            qncp[h8] = sv.n
            C.op("vector", lambda e: e.reduce_max(st[0:1, 36:37], nrow, AX.X),
                 sv, after=(sv,))
            if h8 == 0:
                C.op("vector", lambda e, _g=g: e.tensor_copy(st[0:1, 34 + _g:35 + _g],
                                                             st[0:1, 36:37]), sv, after=(sv,))
            else:
                C.op("vector", lambda e, _g=g: e.tensor_max(st[0:1, 34 + _g:35 + _g],
                                                            st[0:1, 34 + _g:35 + _g],
                                                            st[0:1, 36:37]), sv, after=(sv,))
        if g == 0:
            C.wait("sync", cc, 1)
            C.dma("sync", kgrow.rearrange("a (r k) -> a r k", r=8),
                  knm_gath.ap().rearrange("(a r) k -> a r k", a=1), sdm)
        C.wait("vector", sdm)
        C.op("vector", lambda e: e.reduce_max(st[0:1, 37:38], kgrow, AX.X),
             sv, after=(sv,))
        C.op("vector", lambda e, _g=g: e.tensor_tensor(st[0:1, 38:39], st[0:1, 34 + _g:35 + _g],
                                                       st[0:1, 37:38], ALU.mult), sv, after=(sv,))
        C.wait("scalar", sv)
        C.op("scalar", lambda e: e.sqrt(st[0:1, 39:40], st[0:1, 38:39]), ss)
        C.wait("vector", ss)
        C.op("vector", lambda e: e.tensor_scalar_mul(st[0:1, 39:40], st[0:1, 39:40], -1.0),
             sv, after=(sv,))
        C.op("vector", lambda e: e.tensor_copy(bfs[0:1, 0:1], st[0:1, 39:40]),
             sv, after=(sv,))
        C.wait("tensor", sv)
        C.op("tensor", lambda e: e.matmul(C.ps[6].ap()[:, 0:1], onesr[:, :],
                                          bfs[0:1, 0:1], start=True, stop=True), stt)
        C.wait("vector", stt)
        C.op("vector", lambda e, _g=g: e.tensor_copy(pbias[:, 40 + _g:41 + _g],
                                                     C.ps[6].ap()[:, 0:1]), sv)
        negM = lambda _g=g: pbias[:, 40 + _g:41 + _g]

        P = dict(sv=sv, ss=ss, stt=stt, spe=spe, cc=cc, ps_k=ps_k, ps_v=ps_v,
                 k_gath=k_gath, v_gath=v_gath, kgr=2048, negM=negM, fskv=first_sync_kv)
        for h8 in range(8):
            h = g * 8 + h8
            P["tail"] = emit_attn_head(C, P, h, h8, P.get("tail"))
        P["tail"]()
        P["tail"] = None

    emit_outproj_ln2(C, tag, woT, 16, bo1, ps_wq.fr, pan=True)


def emit_attn1(C, tag, wq, wk, wv, wo, bq1, bk1, bv1, bo1, mtri_in, rope_in,
               k_loc, v_loc, k_gath, v_gath, knm_loc, knm_gath):
    sb = C.sb
    xr, xnf, abf = sb["xr"], sb["xnf"], sb["abf"]
    st, pbias, bfs = sb["stats"], sb["pbias"], sb["bfs"]
    mtri, onesc, onesr, rope = sb["mtri"], sb["onesc"], sb["onesr"], sb["rope"]
    af = sb["af32"]
    QG, KP, VP, PP, WQP, AO = 0, 8192, 10240, 12288, 16896, 20992
    sv = C.sem(tag + "v")
    ss = C.sem(tag + "s")
    stt = C.sem(tag + "t")
    spe = C.sem(tag + "pe")
    cc = C.sem(tag + "cc")
    ksn = C.sem(tag + "ksn")
    ksm = [C.sem(tag + "ks0"), C.sem(tag + "ks1")]
    vsm = [C.sem(tag + f"vs{i}") for i in range(4)]
    sdm = C.sem(tag + "dm")
    sbi = C.sem(tag + "bi")

    emit_ln1_fm(C, list(range(8)), tag, sv, ss, stt)
    C.dma("sync", mtri[:, :], mtri_in, sdm)
    sdr = C.sem(tag + "dr")
    C.dma("gpsimd", rope[:, :], rope_in, sdr)
    C.wait("vector", sdr)

    rowq = af[0:1, 2048:4096]
    C.wait("sync", ss)
    C.wait("sync", sv)
    C.dma("sync", rowq, bq1, sbi)
    C.dma("sync", af[0:1, 4096:4608], bk1, sbi)
    C.dma("sync", af[0:1, 4608:5120], bv1, sbi)
    C.wait("vector", sbi)
    beff = af[0:1, 1536:2048]
    for h in range(4):
        C.op("vector", lambda e, _h=h: e.tensor_add(
            beff[0:1, _h * 128:(_h + 1) * 128],
            rowq[0:1, (4 * _h) * 128:(4 * _h) * 128 + 128],
            rowq[0:1, (4 * _h + 1) * 128:(4 * _h + 1) * 128 + 128]), sv, after=(sv,))
        for g in (2, 3):
            C.op("vector", lambda e, _h=h, _g=g: e.tensor_add(
                beff[0:1, _h * 128:(_h + 1) * 128],
                beff[0:1, _h * 128:(_h + 1) * 128],
                rowq[0:1, (4 * _h + _g) * 128:(4 * _h + _g) * 128 + 128]), sv, after=(sv,))
    C.op("vector", lambda e: e.tensor_copy(bfs[0:1, 0:512], beff), sv, after=(sv,))
    C.op("vector", lambda e: e.tensor_copy(bfs[0:1, 512:1024], af[0:1, 4096:4608]),
         sv, after=(sv,))
    C.op("vector", lambda e: e.tensor_copy(bfs[0:1, 1024:1536], af[0:1, 4608:5120]),
         sv, after=(sv,))
    C.wait("tensor", sv)
    _bm = []
    for i in range(3):
        if i >= 2:
            C.wait("tensor", sv, _bm[i - 2])
        C.op("tensor", lambda e, _i=i: e.matmul(
            C.ps[4 + (_i % 2)].ap(), onesr[:, :], bfs[0:1, _i * 512:(_i + 1) * 512],
            start=True, stop=True), stt)
        C.wait("vector", stt)
        C.op("vector", lambda e, _i=i: e.tensor_copy(
            af32v(C, _i * 512, 512), C.ps[4 + (_i % 2)].ap()), sv)
        _bm.append(C.sems[tag + "v"].n)
    bqe_b = af32v(C, 0, 512)
    bk_b = af32v(C, 512, 512)
    bv_b = af32v(C, 1024, 512)

    ps_w = PanelStream(C, tag + "wp")

    def rope_pass(which, wsrc, ncols, bias_b, tab_off, dst_tm):
        for pre in range(2):
            ps_w.load([(abf[:, WQP + (ps_w.k % 2) * 2048: WQP + (ps_w.k % 2) * 2048 + ncols],
                        wsrc[pre * 128:(pre + 1) * 128, :])])
        qeff_mm = {}
        for dc in range(16):
            if dc + 2 < 16:
                ps_w.load([(abf[:, WQP + (ps_w.k % 2) * 2048: WQP + (ps_w.k % 2) * 2048 + ncols],
                            wsrc[(dc + 2) * 128:(dc + 3) * 128, :])])
            if which == "q":
                ps_w.use("vector")
                wsl = abf[:, WQP + (ps_w.uk % 2) * 2048: WQP + (ps_w.uk % 2) * 2048 + ncols]
                eff = abf[:, PP + (dc % 2) * 512: PP + (dc % 2) * 512 + 512]
                if dc >= 2:
                    C.wait("vector", stt, qeff_mm[dc - 2])
                for hh in range(4):
                    C.op("vector", lambda e, _o=eff[:, hh * 128:(hh + 1) * 128],
                         _a=wsl[:, (4 * hh) * 128:(4 * hh) * 128 + 128],
                         _b=wsl[:, (4 * hh + 1) * 128:(4 * hh + 1) * 128 + 128]:
                         e.tensor_add(_o, _a, _b))
                    for g in (2, 3):
                        C.op("vector", lambda e, _o=eff[:, hh * 128:(hh + 1) * 128],
                             _c=wsl[:, (4 * hh + g) * 128:(4 * hh + g) * 128 + 128]:
                             e.tensor_add(_o, _o, _c),
                             sv if (hh == 3 and g == 3) else None)
                ps_w.done("vector")
                C.wait("tensor", sv)
                for jb in range(8):
                    xt = xnf[:, dc * 1024 + jb * 128: dc * 1024 + jb * 128 + 128]
                    C.op("tensor", lambda e, _b=C.ps[jb], _x=xt, _r=eff,
                         _st=(dc == 0), _sp=(dc == 15):
                         e.matmul(_b.ap(), _x, _r, start=_st, stop=_sp),
                         stt if jb == 7 else None)
                qeff_mm[dc] = stt.n
                continue
            ps_w.use()
            wsl = abf[:, WQP + (ps_w.uk % 2) * 2048: WQP + (ps_w.uk % 2) * 2048 + ncols]
            if dc == 0:
                C.wait("tensor", sv)
            for jb in range(8):
                xt = xnf[:, dc * 1024 + jb * 128: dc * 1024 + jb * 128 + 128]
                last = (jb == 7)
                C.op("tensor", lambda e, _b=C.ps[jb], _x=xt, _r=wsl[:, 0:512],
                     _st=(dc == 0), _sp=(dc == 15):
                     e.matmul(_b.ap(), _x, _r, start=_st, stop=_sp),
                     ps_w.fr if last else None)
            ps_w.done_attached()
        C.wait("vector", ps_w.fr)
        if which == "q":
            C.wait("vector", stt, qeff_mm[15])
        ev = {}
        qt = af[:, 1536:1664]
        ta = af[:, 1664:1728]
        tb = af[:, 1728:1792]
        x1f = qt.rearrange("p (k t) -> p t k", t=2)[:, 0:1, :]
        x2f = qt.rearrange("p (k t) -> p t k", t=2)[:, 1:2, :]
        taf = ta.rearrange("p (o k) -> p o k", o=1)
        tbf = tb.rearrange("p (o k) -> p o k", o=1)
        for jb in range(8):
            if which == "v":
                vp = abf[:, VP + (jb % 4) * 512: VP + (jb % 4) * 512 + 512]
                if jb >= 4:
                    _sm, _vv = vstm[jb - 4]
                    C.wait("vector", _sm, _vv)
                C.op("vector", lambda e, _o=vp, _b=C.ps[jb], _bb=bias_b:
                     e.tensor_add(_o, _b.ap(), _bb), sv, after=(sv,))
                C.wait("sync", sv)
                C.dma("sync", v_loc.ap()[jb * 128:(jb + 1) * 128, :], vp, vsm[jb % 4])
                vstm[jb] = (vsm[jb % 4], vsm[jb % 4].n)
                continue
            for h in range(4):
                pslice = C.ps[jb].ap()[:, h * 128:(h + 1) * 128]
                cqf = rope[:, jb * 256 + tab_off: jb * 256 + tab_off + 64] \
                    .rearrange("p (o k) -> p o k", o=1)
                sqf = rope[:, jb * 256 + tab_off + 64: jb * 256 + tab_off + 128] \
                    .rearrange("p (o k) -> p o k", o=1)
                dst = dst_tm[:, jb * 512 + h * 128: jb * 512 + h * 128 + 128]
                C.op("vector", lambda e, _o=qt, _p=pslice,
                     _bb=bias_b[:, h * 128:(h + 1) * 128]:
                     e.tensor_add(_o, _p, _bb), sv, after=(sv,))
                C.op("vector", lambda e, _a=taf, _x=x1f, _c=cqf:
                     e.tensor_mul(_a, _x, _c), sv, after=(sv,))
                C.op("vector", lambda e, _b2=tbf, _x=x2f, _s=sqf:
                     e.tensor_mul(_b2, _x, _s), sv, after=(sv,))
                C.op("vector", lambda e, _o=dst[:, 0:64], _a=ta, _b2=tb:
                     e.tensor_sub(_o, _a, _b2), sv, after=(sv,))
                C.op("vector", lambda e, _a=taf, _x=x1f, _s=sqf:
                     e.tensor_mul(_a, _x, _s), sv, after=(sv,))
                C.op("vector", lambda e, _b2=tbf, _x=x2f, _c=cqf:
                     e.tensor_mul(_b2, _x, _c), sv, after=(sv,))
                C.op("vector", lambda e, _o=dst[:, 64:128], _a=ta, _b2=tb:
                     e.tensor_add(_o, _a, _b2), sv, after=(sv,))
                ev[(jb, h)] = sv.n
        return ev

    q_tm = abf[:, QG + 4096: QG + 8192]
    vstm = {}
    evq = rope_pass("q", wq, 2048, bqe_b, 0, q_tm)
    tmark = []
    for jb in range(8):
        for h in range(4):
            t = jb * 4 + h
            bank = C.ps[t % 2]
            pv = bank.bitcast(BF16)[:, 0:128]
            need = evq[(0, 3)] if t == 0 else (evq[(1, 3)] if t == 1 else evq[(jb, h)])
            if len(tmark) >= 2:
                need = max(need, tmark[-2])
            C.wait("tensor", sv, need)
            C.op("tensor", lambda e, _p=pv,
                 _i=q_tm[:, jb * 512 + h * 128: jb * 512 + h * 128 + 128]:
                 e.transpose(_p, _i, sb["ident"][:, :]), stt)
            C.wait("vector", stt)
            C.op("vector", lambda e, _o=abf[:, QG + h * 1024 + jb * 128:
                                            QG + h * 1024 + jb * 128 + 128], _p=pv:
                 e.tensor_copy(_o, _p), sv)
            tmark.append(C.sems[tag + "v"].n)
    dmp = af[:, 1536:1664]
    C.wait("scalar", sv)
    for jb in range(8):
        for h in range(4):
            C.op("scalar", lambda e, _q=q_tm[:, jb * 512 + h * 128: jb * 512 + h * 128 + 128],
                 _p=st[:, 16 + jb * 4 + h:17 + jb * 4 + h]:
                 e.activation(dmp, _q, AF.Square, accum_out=_p), ss, after=(ss,))
    C.wait("vector", ss)
    C.op("vector", lambda e: e.reduce_max(st[:, 48:49], st[:, 16:48], AX.X), sv, after=(sv,))
    C.op("vector", lambda e: e.tensor_copy(sb["bcol"][:, 32:33], st[:, 48:49]), sv, after=(sv,))
    C.wait("tensor", sv)
    qnp = C.ps[6].bitcast(BF16)[0:1, 0:128]
    C.op("tensor", lambda e: e.transpose(qnp, sb["bcol"][:, 32:33], sb["ident"][:, :]), stt)
    C.wait("vector", stt)
    C.op("vector", lambda e: e.tensor_copy(af[0:1, 1536:1664], qnp), sv, after=(sv,))
    C.op("vector", lambda e: e.reduce_max(st[0:1, 34:35], af[0:1, 1536:1664], AX.X),
         sv, after=(sv,))

    C.wait("scalar", stt)
    evk = rope_pass("k", wk, 512, bk_b, 128, q_tm)
    ksq = abf[:, PP:PP + 1024]
    tmark = []
    kst = {}
    for h in range(4):
        kp = abf[:, KP + (h % 2) * 1024: KP + (h % 2) * 1024 + 1024]
        if h >= 2:
            _sm, _vv = kst[h - 2]
            C.wait("vector", _sm, _vv)
        for jb in range(8):
            t = h * 8 + jb
            bank = C.ps[t % 2]
            pv = bank.bitcast(BF16)[:, 0:128]
            need = evk[(0, 3)] if t == 0 else (evk[(1, 3)] if t == 1 else evk[(jb, h)])
            if len(tmark) >= 2:
                need = max(need, tmark[-2])
            C.wait("tensor", sv, need)
            C.op("tensor", lambda e, _p=pv,
                 _i=q_tm[:, jb * 512 + h * 128: jb * 512 + h * 128 + 128]:
                 e.transpose(_p, _i, sb["ident"][:, :]), stt)
            C.wait("vector", stt)
            C.op("vector", lambda e, _o=kp[:, jb * 128:(jb + 1) * 128], _p=pv:
                 e.tensor_copy(_o, _p), sv)
            tmark.append(C.sems[tag + "v"].n)
        C.wait("sync", sv)
        C.dma("sync", k_loc.ap()[h * 128:(h + 1) * 128, :], kp, ksm[h % 2])
        kst[h] = (ksm[h % 2], ksm[h % 2].n)
        C.wait("scalar", sv)
        C.op("scalar", lambda e, _kp=kp: e.activation(ksq, _kp, AF.Square), ss, after=(ss,))
        C.wait("tensor", ss)
        for nh in range(2):
            C.op("tensor", lambda e, _r=ksq[:, nh * 512:(nh + 1) * 512],
                 _st=(nh == 0), _sp=(nh == 1):
                 e.matmul(C.ps[5].ap()[0:1, 0:512], onesc[:, :], _r,
                          start=_st, stop=_sp), stt)
        C.wait("vector", stt)
        C.op("vector", lambda e: e.tensor_copy(af[0:1, 2048:2560], C.ps[5].ap()[0:1, 0:512]),
             sv, after=(sv,))
        C.op("vector", lambda e: e.reduce_max(st[0:1, 33:34], af[0:1, 2048:2560], AX.X),
             sv, after=(sv,))
        if h == 0:
            C.op("vector", lambda e: e.tensor_copy(st[0:1, 32:33], st[0:1, 33:34]),
                 sv, after=(sv,))
        else:
            C.op("vector", lambda e: e.tensor_max(st[0:1, 32:33], st[0:1, 32:33],
                                                  st[0:1, 33:34]), sv, after=(sv,))
    knmrow = af[0:1, 2560:2576]
    C.op("vector", lambda e: e.memset(knmrow, 0.0), sv, after=(sv,))
    C.op("vector", lambda e: e.tensor_copy(knmrow[0:1, 0:1], st[0:1, 32:33]), sv, after=(sv,))
    C.wait("sync", sv)
    C.dma("sync", knm_loc.ap(), knmrow, ksn)
    C.wait("gpsimd", ksm[0])
    C.wait("gpsimd", ksm[1])
    C.wait("gpsimd", ksn)
    C.op("gpsimd", lambda e: e.collective_compute(
        "AllGather", ALU.bypass, replica_groups=[list(range(NCORES))],
        ins=[knm_loc.ap()], outs=[knm_gath.ap()]), cc)
    C.op("gpsimd", lambda e: e.collective_compute(
        "AllGather", ALU.bypass, replica_groups=[list(range(NCORES))],
        ins=[k_loc.ap()], outs=[k_gath.ap()]), cc)

    C.wait("scalar", stt)
    rope_pass("v", wv, 512, bv_b, 0, None)
    for _v in vsm:
        C.wait("gpsimd", _v)
    C.op("gpsimd", lambda e: e.collective_compute(
        "AllGather", ALU.bypass, replica_groups=[list(range(NCORES))],
        ins=[v_loc.ap()], outs=[v_gath.ap()]), cc)

    C.wait("sync", cc, 1)
    kgrow = af[0:1, 2688:2816]
    C.dma("sync", kgrow.rearrange("a (r k) -> a r k", r=8),
          knm_gath.ap().rearrange("(a r) k -> a r k", a=1), sdm)
    C.wait("vector", sdm)
    C.op("vector", lambda e: e.reduce_max(st[0:1, 37:38], kgrow, AX.X), sv, after=(sv,))
    C.op("vector", lambda e: e.tensor_tensor(st[0:1, 38:39], st[0:1, 34:35],
                                             st[0:1, 37:38], ALU.mult), sv, after=(sv,))
    C.wait("scalar", sv)
    C.op("scalar", lambda e: e.sqrt(st[0:1, 39:40], st[0:1, 38:39]), ss)
    C.wait("vector", ss)
    C.op("vector", lambda e: e.tensor_scalar_mul(st[0:1, 39:40], st[0:1, 39:40], -1.0),
         sv, after=(sv,))
    C.op("vector", lambda e: e.tensor_copy(bfs[0:1, 0:1], st[0:1, 39:40]), sv, after=(sv,))
    C.wait("tensor", sv)
    C.op("tensor", lambda e: e.matmul(C.ps[6].ap()[:, 0:1], onesr[:, :],
                                      bfs[0:1, 0:1], start=True, stop=True), stt)
    C.wait("vector", stt)
    C.op("vector", lambda e: e.tensor_copy(pbias[:, 42:43], C.ps[6].ap()[:, 0:1]), sv)
    negM = lambda: pbias[:, 42:43]

    ps_k = PanelStream(C, tag + "gk", dma_eng="sync")
    ps_v = PanelStream(C, tag + "gv", dma_eng="sync")
    first_sync_kv = [True]
    P = dict(sv=sv, ss=ss, stt=stt, spe=spe, cc=cc, ps_k=ps_k, ps_v=ps_v,
             k_gath=k_gath, v_gath=v_gath, kgr=512, negM=negM, fskv=first_sync_kv)
    C.wait("tensor", sv)
    C.wait("scalar", stt)
    tail = None
    for h in range(4):
        tail = emit_attn_head(C, P, h, h, tail)
    tail()

    emit_outproj_ln2(C, tag, wo, 4, bo1, ps_w.fr)


def shard_stream(X):
    xb = np.ascontiguousarray(X.reshape(64, 128, D).astype(np.float32))
    return [np.ascontiguousarray(xb[c::8].reshape(TPC, D)) for c in range(NCORES)]


def unshard_stream(outs):
    xb = np.zeros((64, 128, D), np.float32)
    for c in range(NCORES):
        xb[c::8] = np.asarray(outs[c]).reshape(8, 128, D)
    return xb.reshape(1, SEQ, D)


def host_consts():
    ident = np.eye(128, dtype=np.float32)
    h = np.array([[1.0]])
    while h.shape[0] < 128:
        h = np.block([[h, h], [h, -h]])
    h128 = (h / np.sqrt(8192.0)).astype(np.float32)
    return ident, h128


_PROG_CACHE = {}


def build_ffn_program(tag, HD):
    nc = bass.Bass("TRN2", target_bir_lowering=False)
    x_in = nc.dram_tensor("x_in", [TPC, D], F32, kind="ExternalInput").ap()
    gu_w = nc.dram_tensor("gu_w", [HD, 2 * D], F32, kind="ExternalInput").ap()
    d_w = nc.dram_tensor("d_w", [2 * HD, 1024], F32, kind="ExternalInput").ap()
    h128_in = nc.dram_tensor("h128_in", [128, 128], F32, kind="ExternalInput").ap()
    ident_in = nc.dram_tensor("ident_in", [128, 128], F32, kind="ExternalInput").ap()
    x_out = nc.dram_tensor("x_out", [TPC, D], F32, kind="ExternalOutput").ap()

    stack = ExitStack()
    C = Ctx(nc, stack)
    alloc_common(C)
    sdi = C.sem("din")
    sdg = C.sem("ding")
    sdo = C.sem("dout")
    C.dma("sync", C.sb["xr"][:, :].rearrange("p (qb f) -> p qb f", qb=NQB),
          x_in.rearrange("(qb p) f -> p qb f", p=128), sdi)
    C.dma("gpsimd", C.sb["ident"][:, :], ident_in, sdg)
    C.dma("gpsimd", C.sb["h128"][:, :], h128_in, sdg)
    for e in ENGS:
        C.wait(e, sdi)
        C.wait(e, sdg)
    emit_ffn(C, tag, gu_w, d_w, fwht=(tag == "f0"))
    C.wait("sync", C.sems[tag + "v"])
    C.dma("sync", x_out.rearrange("(qb p) f -> p qb f", p=128),
          C.sb["xr"][:, :].rearrange("p (qb f) -> p qb f", qb=NQB), sdo)
    C.wait("sync", sdo)
    C.finalize()
    stack.close()
    return nc


def build_attn0_program(tag):
    nc = bass.Bass("TRN2", target_bir_lowering=False)
    x_in = nc.dram_tensor("x_in", [TPC, D], F32, kind="ExternalInput").ap()
    wq = nc.dram_tensor("wq", [D, D], F32, kind="ExternalInput").ap()
    wk = nc.dram_tensor("wk", [D, D], F32, kind="ExternalInput").ap()
    wv = nc.dram_tensor("wv", [8192, 512], F32, kind="ExternalInput").ap()
    wo = nc.dram_tensor("wo", [8192, 512], F32, kind="ExternalInput").ap()
    bq2 = nc.dram_tensor("bq2", [16, 128], F32, kind="ExternalInput").ap()
    bk2 = nc.dram_tensor("bk2", [16, 128], F32, kind="ExternalInput").ap()
    bv1 = nc.dram_tensor("bv1", [1, D], F32, kind="ExternalInput").ap()
    bo1 = nc.dram_tensor("bo1", [1, D], F32, kind="ExternalInput").ap()
    mtri_in = nc.dram_tensor("mtri_in", [128, 1024], BF16, kind="ExternalInput").ap()
    ident_in = nc.dram_tensor("ident_in", [128, 128], F32, kind="ExternalInput").ap()
    onesc_in = nc.dram_tensor("onesc_in", [128, 1], F32, kind="ExternalInput").ap()
    onesr_in = nc.dram_tensor("onesr_in", [1, 128], F32, kind="ExternalInput").ap()
    x_out = nc.dram_tensor("x_out", [TPC, D], F32, kind="ExternalOutput").ap()

    k_loc = nc.dram_tensor("k_loc", [D, TPC], BF16, kind="Internal")
    v_loc = nc.dram_tensor("v_loc", [TPC, D], BF16, kind="Internal")
    k_gath = nc.dram_tensor("k_gath", [8 * D, TPC], BF16, kind="Internal")
    v_gath = nc.dram_tensor("v_gath", [SEQ, D], BF16, kind="Internal")
    knm_loc = nc.dram_tensor("knm_loc", [1, 16], F32, kind="Internal")
    knm_gath = nc.dram_tensor("knm_gath", [8, 16], F32, kind="Internal")

    stack = ExitStack()
    C = Ctx(nc, stack)
    alloc_common(C)
    sdi = C.sem("din")
    sdg = C.sem("ding")
    sdo = C.sem("dout")
    C.dma("sync", C.sb["xr"][:, :].rearrange("p (qb f) -> p qb f", qb=NQB),
          x_in.rearrange("(qb p) f -> p qb f", p=128), sdi)
    C.dma("gpsimd", C.sb["ident"][:, :], ident_in, sdg)
    C.dma("gpsimd", C.sb["onesc"][:, :], onesc_in, sdg)
    C.dma("gpsimd", C.sb["onesr"][:, :], onesr_in, sdg)
    for e in ENGS:
        C.wait(e, sdi)
        C.wait(e, sdg)
    emit_attn0(C, tag, wq, wk, wv, wo, bq2, bk2, bv1, bo1, mtri_in,
               None, k_loc, v_loc, k_gath, v_gath, knm_loc, knm_gath)
    C.wait("sync", C.sems[tag + "v"])
    C.dma("sync", x_out.rearrange("(qb p) f -> p qb f", p=128),
          C.sb["xr"][:, :].rearrange("p (qb f) -> p qb f", qb=NQB), sdo)
    C.wait("sync", sdo)
    C.finalize()
    stack.close()
    return nc


def pack_gu(wT):
    return np.ascontiguousarray(
        wT.reshape(16, 128, 64, 128).transpose(2, 1, 0, 3).reshape(8192, 2048))


def pack_gu2(gT, uT):
    g = pack_gu(gT).reshape(64, 128, 2048)
    u = pack_gu(uT).reshape(64, 128, 2048)
    return np.ascontiguousarray(np.concatenate([g, u], axis=2).reshape(8192, 4096))


def pack_dn(wT):
    return np.ascontiguousarray(
        wT.reshape(64, 128, 2, 1024).transpose(2, 0, 1, 3).reshape(16384, 1024))


def pack_kq(wT):
    return np.ascontiguousarray(
        wT.reshape(16, 128, 16, 128).transpose(2, 1, 0, 3).reshape(2048, 2048))


def pack_v(wT):
    return np.ascontiguousarray(
        wT.reshape(16, 128, 4, 512).transpose(2, 0, 1, 3).reshape(8192, 512))


def pack_wo(wT, NH=16):
    return np.ascontiguousarray(
        wT.reshape(NH, 128, 4, 512).transpose(2, 0, 1, 3).reshape(4 * NH * 128, 512))


def host_mtri(c):
    m = np.zeros((128, 1024), np.float32)
    for r in range(8):
        if r < c:
            m[:, r * 128:(r + 1) * 128] = 1.0
        elif r == c:
            s = np.arange(128)[:, None]
            q = np.arange(128)[None, :]
            m[:, r * 128:(r + 1) * 128] = (q >= s).astype(np.float32)
    return m.astype(ml_dtypes.bfloat16)


def run_attn0_launch(x_shards, Wq, bq, Wk, bk, Wv, bv, Wo, bo):
    key = ("a0",)
    if key not in _PROG_CACHE:
        _PROG_CACHE[key] = build_attn0_program("a0")
    nc = _PROG_CACHE[key]
    ident, _ = host_consts()
    onesc = np.ones((128, 1), np.float32)
    onesr = np.ones((1, 128), np.float32)
    wqT = pack_kq(Wq.T)
    wkT = pack_kq(Wk.T)
    wvT = pack_v(Wv.T)
    woT = pack_wo(Wo.T, 16)
    in_maps = []
    for c in range(NCORES):
        in_maps.append(dict(
            x_in=x_shards[c], wq=wqT, wk=wkT, wv=wvT, wo=woT,
            bq2=np.ascontiguousarray(bq.reshape(16, 128)),
            bk2=np.ascontiguousarray(bk.reshape(16, 128)),
            bv1=np.ascontiguousarray(bv.reshape(1, D)),
            bo1=np.ascontiguousarray(bo.reshape(1, D)),
            mtri_in=host_mtri(c), ident_in=ident,
            onesc_in=onesc, onesr_in=onesr))
    res = run_bass_kernel_spmd(nc, in_maps, core_ids=list(range(NCORES)))
    return [res.results[c]["x_out"] for c in range(NCORES)]


def build_attn1_program(tag):
    nc = bass.Bass("TRN2", target_bir_lowering=False)
    x_in = nc.dram_tensor("x_in", [TPC, D], F32, kind="ExternalInput").ap()
    wq = nc.dram_tensor("wq", [D, D], F32, kind="ExternalInput").ap()
    wk = nc.dram_tensor("wk", [D, 512], F32, kind="ExternalInput").ap()
    wv = nc.dram_tensor("wv", [D, 512], F32, kind="ExternalInput").ap()
    wo = nc.dram_tensor("wo", [512, D], F32, kind="ExternalInput").ap()
    bq1 = nc.dram_tensor("bq1", [1, D], F32, kind="ExternalInput").ap()
    bk1 = nc.dram_tensor("bk1", [1, 512], F32, kind="ExternalInput").ap()
    bv1 = nc.dram_tensor("bv1", [1, 512], F32, kind="ExternalInput").ap()
    bo1 = nc.dram_tensor("bo1", [1, D], F32, kind="ExternalInput").ap()
    mtri_in = nc.dram_tensor("mtri_in", [128, 1024], BF16, kind="ExternalInput").ap()
    rope_in = nc.dram_tensor("rope_in", [128, 2048], F32, kind="ExternalInput").ap()
    ident_in = nc.dram_tensor("ident_in", [128, 128], F32, kind="ExternalInput").ap()
    onesc_in = nc.dram_tensor("onesc_in", [128, 1], F32, kind="ExternalInput").ap()
    onesr_in = nc.dram_tensor("onesr_in", [1, 128], F32, kind="ExternalInput").ap()
    x_out = nc.dram_tensor("x_out", [TPC, D], F32, kind="ExternalOutput").ap()

    k_loc = nc.dram_tensor("k_loc", [512, TPC], BF16, kind="Internal")
    v_loc = nc.dram_tensor("v_loc", [TPC, 512], BF16, kind="Internal")
    k_gath = nc.dram_tensor("k_gath", [8 * 512, TPC], BF16, kind="Internal")
    v_gath = nc.dram_tensor("v_gath", [SEQ, 512], BF16, kind="Internal")
    knm_loc = nc.dram_tensor("knm_loc", [1, 16], F32, kind="Internal")
    knm_gath = nc.dram_tensor("knm_gath", [8, 16], F32, kind="Internal")

    stack = ExitStack()
    C = Ctx(nc, stack)
    alloc_common(C)
    sdi = C.sem("din")
    sdg = C.sem("ding")
    sdo = C.sem("dout")
    C.dma("sync", C.sb["xr"][:, :].rearrange("p (qb f) -> p qb f", qb=NQB),
          x_in.rearrange("(qb p) f -> p qb f", p=128), sdi)
    C.dma("gpsimd", C.sb["ident"][:, :], ident_in, sdg)
    C.dma("gpsimd", C.sb["onesc"][:, :], onesc_in, sdg)
    C.dma("gpsimd", C.sb["onesr"][:, :], onesr_in, sdg)
    for e in ENGS:
        C.wait(e, sdi)
        C.wait(e, sdg)
    emit_attn1(C, tag, wq, wk, wv, wo, bq1, bk1, bv1, bo1, mtri_in, rope_in,
               k_loc, v_loc, k_gath, v_gath, knm_loc, knm_gath)
    C.wait("sync", C.sems[tag + "v"])
    C.dma("sync", x_out.rearrange("(qb p) f -> p qb f", p=128),
          C.sb["xr"][:, :].rearrange("p (qb f) -> p qb f", qb=NQB), sdo)
    C.wait("sync", sdo)
    C.finalize()
    stack.close()
    return nc


def host_rope(c):
    out = np.zeros((128, 2048), np.float32)
    inv = 1.0 / 10000.0 ** (np.arange(64) * 2.0 / 128)
    s1 = 1.0 / np.sqrt(128.0)
    for jb in range(8):
        pos = (8 * jb + c) * 128 + np.arange(128)
        emb = pos[:, None] * inv[None, :]
        out[:, jb * 256 + 0:jb * 256 + 64] = np.cos(emb) * s1
        out[:, jb * 256 + 64:jb * 256 + 128] = np.sin(emb) * s1
        out[:, jb * 256 + 128:jb * 256 + 192] = np.cos(emb)
        out[:, jb * 256 + 192:jb * 256 + 256] = np.sin(emb)
    return out


def run_attn1_launch(x_shards, Wq, bq, Wk, bk, Wv, bv, Wo, bo):
    key = ("a1",)
    if key not in _PROG_CACHE:
        _PROG_CACHE[key] = build_attn1_program("a1")
    nc = _PROG_CACHE[key]
    ident, _ = host_consts()
    onesc = np.ones((128, 1), np.float32)
    onesr = np.ones((1, 128), np.float32)
    in_maps = []
    for c in range(NCORES):
        in_maps.append(dict(
            x_in=x_shards[c], wq=np.ascontiguousarray(Wq),
            wk=np.ascontiguousarray(Wk), wv=np.ascontiguousarray(Wv),
            wo=np.ascontiguousarray(Wo),
            bq1=np.ascontiguousarray(bq.reshape(1, D)),
            bk1=np.ascontiguousarray(bk.reshape(1, 512)),
            bv1=np.ascontiguousarray(bv.reshape(1, 512)),
            bo1=np.ascontiguousarray(bo.reshape(1, D)),
            mtri_in=host_mtri(c), rope_in=host_rope(c), ident_in=ident,
            onesc_in=onesc, onesr_in=onesr))
    res = run_bass_kernel_spmd(nc, in_maps, core_ids=list(range(NCORES)))
    return [res.results[c]["x_out"] for c in range(NCORES)]


def run_ffn_launch(tag, x_shards, g_w, u_w, d_w):
    HD = g_w.shape[1]
    key = (tag, HD)
    if key not in _PROG_CACHE:
        _PROG_CACHE[key] = build_ffn_program(tag, HD)
    nc = _PROG_CACHE[key]
    ident, h128 = host_consts()
    in_maps = [dict(x_in=x_shards[c], g_w=g_w, u_w=u_w, d_w=d_w,
                    h128_in=h128, ident_in=ident) for c in range(NCORES)]
    res = run_bass_kernel_spmd(nc, in_maps, core_ids=list(range(NCORES)))
    return [res.results[c]["x_out"] for c in range(NCORES)]


def phase_barrier(C, tags):
    for e in ENGS:
        for name, s in list(C.sems.items()):
            if any(name.startswith(t) for t in tags) and s.n > 0:
                C.wait(e, s)


def build_fused_program():
    nc = bass.Bass("TRN2", target_bir_lowering=False)
    T = lambda n, sh: nc.dram_tensor(n, sh, F32, kind="ExternalInput").ap()
    x_in = T("x_in", [TPC, D])
    a0wq, a0wk = T("a0wq", [D, D]), T("a0wk", [D, D])
    a0wv, a0wo = T("a0wv", [8192, 512]), T("a0wo", [8192, 512])
    a0bq, a0bk = T("a0bq", [16, 128]), T("a0bk", [16, 128])
    a0bv, a0bo = T("a0bv", [1, D]), T("a0bo", [1, D])
    f0gu = T("f0gu", [8192, 4096])
    f0d = T("f0d", [16384, 1024])
    a1wq = T("a1wq", [D, D])
    a1wk, a1wv = T("a1wk", [D, 512]), T("a1wv", [D, 512])
    a1wo = T("a1wo", [512, D])
    a1bq, a1bk = T("a1bq", [1, D]), T("a1bk", [1, 512])
    a1bv, a1bo = T("a1bv", [1, 512]), T("a1bo", [1, D])
    f1gu = T("f1gu", [8192, 4096])
    f1d = T("f1d", [16384, 1024])
    mtri_in = nc.dram_tensor("mtri_in", [128, 1024], BF16, kind="ExternalInput").ap()
    rope_in = T("rope_in", [128, 2048])
    ident_in = T("ident_in", [128, 128])
    onesc_in = T("onesc_in", [128, 1])
    onesr_in = T("onesr_in", [1, 128])
    h128_in = T("h128_in", [128, 128])
    x_out = nc.dram_tensor("x_out", [TPC, D], F32, kind="ExternalOutput").ap()

    k_loc0 = nc.dram_tensor("k_loc0", [D, TPC], BF16, kind="Internal")
    v_loc0 = nc.dram_tensor("v_loc0", [TPC, D], BF16, kind="Internal")
    k_gath0 = nc.dram_tensor("k_gath0", [8 * D, TPC], BF16, kind="Internal")
    v_gath0 = nc.dram_tensor("v_gath0", [SEQ, D], BF16, kind="Internal")
    knm_loc0 = nc.dram_tensor("knm_loc0", [1, 16], F32, kind="Internal")
    knm_gath0 = nc.dram_tensor("knm_gath0", [8, 16], F32, kind="Internal")
    k_loc1 = nc.dram_tensor("k_loc1", [512, TPC], BF16, kind="Internal")
    v_loc1 = nc.dram_tensor("v_loc1", [TPC, 512], BF16, kind="Internal")
    k_gath1 = nc.dram_tensor("k_gath1", [8 * 512, TPC], BF16, kind="Internal")
    v_gath1 = nc.dram_tensor("v_gath1", [SEQ, 512], BF16, kind="Internal")
    knm_loc1 = nc.dram_tensor("knm_loc1", [1, 16], F32, kind="Internal")
    knm_gath1 = nc.dram_tensor("knm_gath1", [8, 16], F32, kind="Internal")

    stack = ExitStack()
    C = Ctx(nc, stack)
    alloc_common(C)
    sdi = C.sem("din")
    sdg = C.sem("ding")
    sdo = C.sem("dout")
    C.dma("sync", C.sb["xr"][:, :].rearrange("p (qb f) -> p qb f", qb=NQB),
          x_in.rearrange("(qb p) f -> p qb f", p=128), sdi)
    C.dma("gpsimd", C.sb["ident"][:, :], ident_in, sdg)
    C.dma("gpsimd", C.sb["onesc"][:, :], onesc_in, sdg)
    C.dma("gpsimd", C.sb["onesr"][:, :], onesr_in, sdg)
    C.dma("gpsimd", C.sb["h128"][:, :], h128_in, sdg)
    for e in ENGS:
        C.wait(e, sdi)
        C.wait(e, sdg)
    emit_attn0(C, "a0", a0wq, a0wk, a0wv, a0wo, a0bq, a0bk, a0bv, a0bo, mtri_in,
               None, k_loc0, v_loc0, k_gath0, v_gath0, knm_loc0, knm_gath0)
    phase_barrier(C, ("a0",))
    emit_ffn(C, "f0", f0gu, f0d, fwht=True)
    phase_barrier(C, ("f0",))
    emit_attn1(C, "a1", a1wq, a1wk, a1wv, a1wo, a1bq, a1bk, a1bv, a1bo, mtri_in,
               rope_in, k_loc1, v_loc1, k_gath1, v_gath1, knm_loc1, knm_gath1)
    phase_barrier(C, ("a1",))
    emit_ffn(C, "f1", f1gu, f1d, fwht=False)
    phase_barrier(C, ("f1",))
    C.dma("sync", x_out.rearrange("(qb p) f -> p qb f", p=128),
          C.sb["xr"][:, :].rearrange("p (qb f) -> p qb f", qb=NQB), sdo)
    C.wait("sync", sdo)
    C.finalize()
    stack.close()
    return nc


def kernel_fused(g):
    key = ("fused",)
    if key not in _PROG_CACHE:
        _PROG_CACHE[key] = build_fused_program()
    nc = _PROG_CACHE[key]
    ident, h128 = host_consts()
    onesc = np.ones((128, 1), np.float32)
    onesr = np.ones((1, 128), np.float32)
    shards = shard_stream(g["L0_attn_X"])
    ct = np.ascontiguousarray
    shared = dict(
        a0wq=pack_kq(g["L0_attn_Wq"].T), a0wk=pack_kq(g["L0_attn_Wk"].T),
        a0wv=pack_v(g["L0_attn_Wv"].T), a0wo=pack_wo(g["L0_attn_Wo"].T, 16),
        a0bq=ct(g["L0_attn_bq"].reshape(16, 128)),
        a0bk=ct(g["L0_attn_bk"].reshape(16, 128)),
        a0bv=ct(g["L0_attn_bv"].reshape(1, D)),
        a0bo=ct(g["L0_attn_bo"].reshape(1, D)),
        f0gu=pack_gu2(g["L0_ffn_W1"].T, g["L0_ffn_W3"].T),
        f0d=pack_dn(g["L0_ffn_W2"].T),
        a1wq=ct(g["L1_attn_Wq"]), a1wk=ct(g["L1_attn_Wk"]),
        a1wv=ct(g["L1_attn_Wv"]), a1wo=ct(g["L1_attn_Wo"]),
        a1bq=ct(g["L1_attn_bq"].reshape(1, D)),
        a1bk=ct(g["L1_attn_bk"].reshape(1, 512)),
        a1bv=ct(g["L1_attn_bv"].reshape(1, 512)),
        a1bo=ct(g["L1_attn_bo"].reshape(1, D)),
        f1gu=pack_gu2(g["L1_ffn_w_gate_up"][:8192].T,
                      g["L1_ffn_w_gate_up"][8192:].T),
        f1d=pack_dn(g["L1_ffn_w_down"].T),
        ident_in=ident, onesc_in=onesc, onesr_in=onesr, h128_in=h128)
    in_maps = []
    for c in range(NCORES):
        m = dict(shared)
        m["x_in"] = shards[c]
        m["mtri_in"] = host_mtri(c)
        m["rope_in"] = host_rope(c)
        in_maps.append(m)
    res = run_bass_kernel_spmd(nc, in_maps, core_ids=list(range(NCORES)))
    return unshard_stream([res.results[c]["x_out"] for c in range(NCORES)])


FUSED = True


def kernel(**inputs):
    g = {k: np.ascontiguousarray(np.asarray(v, np.float32))
         for k, v in inputs.items()}
    if FUSED:
        return kernel_fused(g)
    shards = shard_stream(g["L0_attn_X"])
    shards = run_attn0_launch(shards, g["L0_attn_Wq"], g["L0_attn_bq"],
                              g["L0_attn_Wk"], g["L0_attn_bk"],
                              g["L0_attn_Wv"], g["L0_attn_bv"],
                              g["L0_attn_Wo"], g["L0_attn_bo"])
    shards = [np.ascontiguousarray(np.asarray(s)) for s in shards]
    shards = run_ffn_launch("f0", shards,
                            np.ascontiguousarray(g["L0_ffn_W1"].T),
                            np.ascontiguousarray(g["L0_ffn_W3"].T),
                            np.ascontiguousarray(g["L0_ffn_W2"].T))
    shards = [np.ascontiguousarray(np.asarray(s)) for s in shards]
    shards = run_attn1_launch(shards, g["L1_attn_Wq"], g["L1_attn_bq"],
                              g["L1_attn_Wk"], g["L1_attn_bk"],
                              g["L1_attn_Wv"], g["L1_attn_bv"],
                              g["L1_attn_Wo"], g["L1_attn_bo"])
    shards = [np.ascontiguousarray(np.asarray(s)) for s in shards]
    shards = run_ffn_launch("f1", shards,
                            np.ascontiguousarray(g["L1_ffn_w_gate_up"][:8192].T),
                            np.ascontiguousarray(g["L1_ffn_w_gate_up"][8192:].T),
                            np.ascontiguousarray(g["L1_ffn_w_down"].T))
    return unshard_stream(shards)
